# version 12
# baseline (speedup 1.0000x reference)
import contextlib
import numpy as np

import concourse.bass as bass
import concourse.mybir as mybir
from concourse.bass_utils import run_bass_kernel_spmd

F32 = mybir.dt.float32
BF16 = mybir.dt.bfloat16
AF = mybir.ActivationFunctionType
ALU = mybir.AluOpType
AX = mybir.AxisListType

B, S, D = 2, 8192, 1024
NH, DK = 16, 64
NE, HF = 8, 2048
NC = 8
OWN = 2048
NT = S // 128
NOT = OWN // 128
NCH = D // 128
NHC = HF // 128
NG = NT // 4


class Sem:
    def __init__(self, ctx, nc, name):
        self.h = ctx.enter_context(nc.semaphore(name))
        self.n = 0

    def inc(self, instr, k=1):
        instr.then_inc(self.h, k)
        self.n += k
        return self.n

    def dma(self, eng, dst, src):
        eng.wait_ge(self.h, self.n)
        ins = eng.dma_start(dst, src)
        return self.inc(ins, 16)


def build_nc(phases=4):
    nc = bass.Bass("TRN2", target_bir_lowering=False)

    x = nc.dram_tensor("x", [S, D], F32, kind="ExternalInput")
    wq = nc.dram_tensor("wq", [D, DK], F32, kind="ExternalInput")
    wk = nc.dram_tensor("wk", [D, DK], F32, kind="ExternalInput")
    wv = nc.dram_tensor("wv", [D, DK], F32, kind="ExternalInput")
    fcq_w = nc.dram_tensor("fcq_w", [DK, D], F32, kind="ExternalInput")
    fck_w = nc.dram_tensor("fck_w", [DK, D], F32, kind="ExternalInput")
    fcv_w = nc.dram_tensor("fcv_w", [DK, D], F32, kind="ExternalInput")
    fcq_b = nc.dram_tensor("fcq_b", [D], F32, kind="ExternalInput")
    fck_b = nc.dram_tensor("fck_b", [D], F32, kind="ExternalInput")
    fcv_b = nc.dram_tensor("fcv_b", [D], F32, kind="ExternalInput")
    fco_w = nc.dram_tensor("fco_w", [D, D], F32, kind="ExternalInput")
    fco_b = nc.dram_tensor("fco_b", [D], F32, kind="ExternalInput")
    w_router = nc.dram_tensor("w_router", [D, NE], F32, kind="ExternalInput")
    b_router = nc.dram_tensor("b_router", [NE], F32, kind="ExternalInput")
    w1 = nc.dram_tensor("w1", [NE, D, HF], F32, kind="ExternalInput")
    b1 = nc.dram_tensor("b1", [NE, HF], F32, kind="ExternalInput")
    w2 = nc.dram_tensor("w2", [NE, HF, D], F32, kind="ExternalInput")
    b2 = nc.dram_tensor("b2", [NE, D], F32, kind="ExternalInput")
    out = nc.dram_tensor("out", [OWN, D], F32, kind="ExternalOutput")
    attnT_hbm = nc.dram_tensor("attnT_hbm", [D, OWN], BF16, kind="Internal")

    xp = x.rearrange("(n p) f -> p n f", p=128)

    PE, ACT, DVE, SP, GP = nc.tensor, nc.scalar, nc.vector, nc.sync, nc.gpsimd

    with contextlib.ExitStack() as ctx:
        E = ctx.enter_context

        ident = E(nc.sbuf_tensor("ident", [128, 128], F32))
        ones1b = E(nc.sbuf_tensor("ones1b", [1, 128], BF16))
        ones1f = E(nc.sbuf_tensor("ones1f", [1, 128], F32))
        ones64b = E(nc.sbuf_tensor("ones64b", [64, 1], BF16))
        eps_t = E(nc.sbuf_tensor("eps_t", [128, 1], F32))
        xnT_own = E(nc.sbuf_tensor("xnT_own", [128, NCH, OWN], BF16))
        W_sb = E(nc.sbuf_tensor("W_sb", [128, NOT, NE], F32))
        b1T = E(nc.sbuf_tensor("b1T", [128, 128], F32))
        b1_raw = E(nc.sbuf_tensor("b1_raw", [128, 128], F32))
        brout_row = E(nc.sbuf_tensor("brout_row", [1, NE], F32))

        psA = E(nc.psum_tensor("psA", [128, 2048], F32))
        psB = E(nc.psum_tensor("psB", [128, 2048], F32))

        dS = Sem(ctx, nc, "dS")
        dG = Sem(ctx, nc, "dG")
        dO = Sem(ctx, nc, "dO")
        sP = Sem(ctx, nc, "sP")
        sA = Sem(ctx, nc, "sA")
        sV = Sem(ctx, nc, "sV")
        sG = Sem(ctx, nc, "sG")

        blk = E(nc.Block())

        GP.memset(ident[:, :], 0.0)
        GP.drain()
        GP.affine_select(out=ident[:, :], in_=ident[:, :],
                         compare_op=ALU.not_equal, fill=1.0, base=0,
                         pattern=[[-1, 128]], channel_multiplier=1)
        GP.memset(ones1b[:, :], 1.0)
        GP.memset(ones1f[:, :], 1.0)
        GP.memset(ones64b[:, :], 1.0)
        GP.memset(eps_t[:, :], 1e-5)

        dS.dma(SP, brout_row[:, :], b_router[None, :])
        dS.dma(SP, b1_raw[:, :], b1.rearrange("e (q a) -> (e q) a", a=128))
        ev_small = dS.n

        with contextlib.ExitStack() as ab:
            EA = ab.enter_context
            t1kT = EA(nc.sbuf_tensor("t1kT", [128, S], BF16))
            t1vT = EA(nc.sbuf_tensor("t1vT", [128, S], BF16))
            t1qT = EA(nc.sbuf_tensor("t1qT", [128, OWN], BF16))
            fcq_aug = EA(nc.sbuf_tensor("fcq_aug", [128, D], BF16))
            fck_aug = EA(nc.sbuf_tensor("fck_aug", [128, D], BF16))
            fcv_aug = EA(nc.sbuf_tensor("fcv_aug", [128, D], BF16))
            GP.memset(t1kT[64:128, :], 0.0)
            GP.memset(t1vT[64:128, :], 0.0)
            GP.memset(t1qT[64:128, :], 0.0)
            GP.memset(fcq_aug[64:128, :], 0.0)
            GP.memset(fck_aug[64:128, :], 0.0)
            GP.memset(fcv_aug[64:128, :], 0.0)
            kT_2 = EA(nc.sbuf_tensor("kT_2", [128, 2, S], BF16))
            qT_2 = EA(nc.sbuf_tensor("qT_2", [128, 2, OWN], BF16))
            V_pair = EA(nc.sbuf_tensor("V_pair", [128, NT, 130], BF16))
            fcv_pack = EA(nc.sbuf_tensor("fcv_pack", [128, 8, 130], BF16))
            GP.memset(kT_2[64:128, :, :], 0.0)
            GP.memset(qT_2[64:128, :, :], 0.0)
            GP.drain()
            GP.memset(t1kT[64:65, :], 1.0)
            GP.memset(t1vT[64:65, :], 1.0)
            GP.memset(kT_2[64:65, :, :], -1.0)
            ev_consts = sG.inc(GP.memset(t1qT[64:65, :], 1.0))

            with contextlib.ExitStack() as aa:
                EB = aa.enter_context
                wqkv_bf = EB(nc.sbuf_tensor("wqkv_bf", [128, NCH, 4 * DK], BF16))
                w_rout_f = EB(nc.sbuf_tensor("w_rout_f", [128, NCH, NE], F32))
                xg = EB(nc.sbuf_tensor("xg", [128, 2, D], F32))
                xn_f = EB(nc.sbuf_tensor("xn_f", [128, 2, D], F32))
                xnT_st = EB(nc.sbuf_tensor("xnT_st", [128, 2, NCH, 512], BF16))
                xnTf_st = EB(nc.sbuf_tensor("xnTf_st", [128, 2, NCH, 128], F32))
                mvst = EB(nc.sbuf_tensor("mvst", [128, 12], F32))
                lnt = EB(nc.sbuf_tensor("lnt", [128, 2, 6], F32))
                xc_f = EB(nc.sbuf_tensor("xc_f", [128, 2, D], F32))
                rl_sc = EB(nc.sbuf_tensor("rl_sc", [128, 8, NE], F32))
                rst = EB(nc.sbuf_tensor("rst", [128, 4], F32))

                GP.memset(wqkv_bf[:, :, DK:2 * DK], 0.0)
                GP.memset(fcv_pack[:, :, 64:65], 0.0)
                GP.memset(fcv_pack[:, :, 129:130], 0.0)
                GP.drain()
                GP.memset(fcv_pack[64:65, :, 64:65], 1.0)
                ev_fvp0 = sG.inc(GP.memset(fcv_pack[64:65, :, 129:130], 1.0))
                dG.dma(GP, wqkv_bf[:, :, 0:DK], wq.rearrange("(c p) k -> p c k", p=128))
                dG.dma(GP, wqkv_bf[:, :, 2 * DK:3 * DK], wk.rearrange("(c p) k -> p c k", p=128))
                dG.dma(GP, wqkv_bf[:, :, 3 * DK:4 * DK], wv.rearrange("(c p) k -> p c k", p=128))
                ev_wqkv = dG.n
                for wsrc, bsrc, dst in ((fcq_w, fcq_b, fcq_aug), (fck_w, fck_b, fck_aug),
                                        (fcv_w, fcv_b, fcv_aug)):
                    dG.dma(GP, dst[0:64, :], wsrc[:, :])
                    dG.dma(GP, dst[64:65, :], bsrc[None, :])
                ev_wA = dG.n
                ev_wrout = dS.dma(SP, w_rout_f[:, :, :],
                                  w_router.rearrange("(c p) e -> p c e", p=128))

                ev_xdma = [0] * NT
                ev_xn = [0] * NT
                ev_sub = [0] * NT
                ev_tp = [0] * NT
                ev_evac = [0] * NT
                ev_evacf = [0] * NT
                ev_t1 = [0] * NG
                ev_t1ev = [0] * NG
                ev_rlog = [0] * NOT
                ev_rW = [0] * NOT

                def stage1(t):
                    own = t < NOT
                    if t == 0:
                        ACT.wait_ge(sG.h, ev_consts)
                    if t >= 2:
                        SP.wait_ge(sV.h, ev_sub[t - 2])
                    ev_xdma[t] = dS.dma(SP, xg[:, t % 2, :], xp[:, t, :])
                    xtile = xg[:, t % 2, :]
                    lb = lnt[:, t % 2, :]
                    DVE.wait_ge(dS.h, ev_xdma[t])
                    if t >= 2:
                        DVE.wait_ge(sA.h, ev_xn[t - 2])
                    if t > 0:
                        DVE.drain()
                    DVE.bn_stats(mvst[:, 0:6], xtile[:, 0:512])
                    DVE.bn_stats(mvst[:, 6:12], xtile[:, 512:1024])
                    DVE.drain()
                    ev1 = sV.inc(DVE.bn_aggr(lb[:, 0:2], mvst[:, 0:12]))
                    DVE.drain()
                    ev_sub[t] = sV.inc(DVE.tensor_scalar(
                        out=xc_f[:, t % 2, :], in0=xtile,
                        scalar1=lb[:, 0:1], scalar2=None, op0=ALU.subtract))
                    ACT.wait_ge(sV.h, ev1)
                    if t > 0:
                        ACT.drain()
                    ACT.activation(lb[:, 4:5], lb[:, 1:2], AF.Ln, bias=eps_t[:, :])
                    ACT.drain()
                    ACT.activation(lb[:, 2:3], lb[:, 4:5], AF.Exp, scale=-0.5)
                    ACT.drain()
                    ACT.wait_ge(sV.h, ev_sub[t])
                    ev_xn[t] = sA.inc(ACT.activation(xn_f[:, t % 2, :],
                                                     xc_f[:, t % 2, :],
                                                     AF.Identity, scale=lb[:, 2:3]))

                def stage2(t):
                    own = t < NOT
                    g4 = t // 4
                    if t == 0:
                        PE.wait_ge(sG.h, ev_consts)
                    PE.wait_ge(sA.h, ev_xn[t])
                    if t >= 2:
                        if t - 2 < NOT:
                            PE.wait_ge(sA.h, ev_evacf[t - 2])
                        else:
                            PE.wait_ge(sV.h, ev_evac[t - 2])
                    base = (t % 2) * 1024
                    for c in range(NCH):
                        off = base + (c // 4) * 512 + (c % 4) * 128
                        ins = PE.transpose(psA[:, off:off + 128],
                                           xn_f[:, t % 2, c * 128:(c + 1) * 128],
                                           ident[:, :])
                    ev_tp[t] = sP.inc(ins)
                    if own:
                        ACT.wait_ge(sP.h, ev_tp[t])
                        for j in range(2):
                            ins = ACT.activation(
                                xnTf_st[:, t % 2, j * 4:(j + 1) * 4, :],
                                psA[:, base + j * 512: base + j * 512 + 512], AF.Copy)
                        ev_evacf[t] = sA.inc(ins)
                        DVE.wait_ge(sA.h, ev_evacf[t])
                        ins = DVE.tensor_copy(
                            xnT_own[:, :, t * 128:(t + 1) * 128],
                            xnTf_st[:, t % 2, :, :])
                        ev_evac[t] = sV.inc(ins)
                    else:
                        DVE.wait_ge(sP.h, ev_tp[t])
                        if g4 >= 2:
                            DVE.wait_ge(sP.h, ev_t1[g4 - 2])
                        for j in range(2):
                            srcp = psA[:, base + j * 512: base + j * 512 + 512]
                            dst = xnT_st[:, g4 % 2, j * 4:(j + 1) * 4,
                                         (t % 4) * 128:(t % 4) * 128 + 128]
                            ins = DVE.tensor_copy(dst, srcp)
                        ev_evac[t] = sV.inc(ins)
                        ev_evacf[t] = 0
                    if t % 4 == 3:
                        PE.wait_ge(sV.h, ev_evac[t])
                        if g4 == 0:
                            PE.wait_ge(dG.h, ev_wqkv)
                        if g4 >= 1:
                            PE.wait_ge(sA.h, ev_t1ev[g4 - 1])
                        projs = [(2, psB[:, 0:512])]
                        if own:
                            projs.append((0, psB[:, 512:1024]))
                        for wi, ps in projs:
                            for c in range(NCH):
                                if own:
                                    rhs = xnT_own[:, c, g4 * 512:(g4 + 1) * 512]
                                else:
                                    rhs = xnT_st[:, g4 % 2, c, :]
                                ins = PE.matmul(ps, wqkv_bf[:, c, wi * DK:(wi + 2) * DK],
                                                rhs, start=(c == 0), stop=(c == NCH - 1))
                        ev_t1[g4] = sP.inc(ins)
                        ACT.wait_ge(sP.h, ev_t1[g4])
                        ins = ACT.activation(t1kT[0:64, g4 * 512:(g4 + 1) * 512],
                                             psB[0:64, 0:512], AF.Copy)
                        ins = ACT.activation(t1vT[0:64, g4 * 512:(g4 + 1) * 512],
                                             psB[64:128, 0:512], AF.Copy)
                        if own:
                            ins = ACT.activation(t1qT[0:64, g4 * 512:(g4 + 1) * 512],
                                                 psB[0:64, 512:1024], AF.Copy)
                        ev_t1ev[g4] = sA.inc(ins)

                def stage3(t):
                    slot = t % 4
                    rps = psB[:, 1536 + slot * NE: 1536 + (slot + 1) * NE]
                    PE.wait_ge(sA.h, ev_evacf[t])
                    if t == 0:
                        PE.wait_ge(dS.h, ev_wrout)
                    else:
                        PE.wait_ge(sV.h, ev_rlog[t - 1])
                    PE.matmul(rps, ones1f[:, :], brout_row[:, :], start=True, stop=False)
                    for c in range(NCH):
                        ins = PE.matmul(rps, xnTf_st[:, t % 2, c, :], w_rout_f[:, c, :],
                                        start=False, stop=(c == NCH - 1))
                    evr = sP.inc(ins)
                    DVE.wait_ge(sP.h, evr)
                    if t >= 4:
                        DVE.wait_ge(sV.h, ev_rW[t - 4])
                    DVE.drain()
                    DVE.tensor_copy(rl_sc[:, slot, :], rps)
                    DVE.drain()
                    DVE.reduce_max(rst[:, 0:1], rl_sc[:, slot, :], axis=AX.X)
                    DVE.drain()
                    ev_rlog[t] = sV.inc(DVE.tensor_scalar(
                        out=rst[:, 1:2], in0=rst[:, 0:1],
                        scalar1=-1.0, scalar2=None, op0=ALU.mult))
                    ACT.wait_ge(sV.h, ev_rlog[t])
                    ACT.drain()
                    ev5 = sA.inc(ACT.activation(rl_sc[:, 4 + slot, :], rl_sc[:, slot, :],
                                                AF.Exp, bias=rst[:, 1:2]))
                    DVE.wait_ge(sA.h, ev5)
                    DVE.drain()
                    DVE.reduce_sum(rst[:, 2:3], rl_sc[:, 4 + slot, :], axis=AX.X)
                    DVE.drain()
                    DVE.reciprocal(rst[:, 3:4], rst[:, 2:3])
                    DVE.drain()
                    ev_rW[t] = sV.inc(DVE.tensor_scalar(
                        out=W_sb[:, t, :], in0=rl_sc[:, slot, :],
                        scalar1=rst[:, 0:1], scalar2=rst[:, 3:4],
                        op0=ALU.is_equal, op1=ALU.mult))

                for t in range(NT + 2):
                    if t < NT:
                        stage1(t)
                    if 1 <= t <= NT:
                        stage2(t - 1)
                    if 2 <= t and t - 2 < NOT:
                        stage3(t - 2)

                last_t1ev = ev_t1ev[NG - 1]
                last_A_evac = ev_evac[NT - 1]

                if phases in (1, 9, 10, 11, 12, 13):
                    if phases in (9, 10):
                        SP.wait_ge(sA.h, sA.n)
                        SP.wait_ge(sV.h, sV.n)
                        return nc
                    SP.wait_ge(sA.h, sA.n)
                    SP.wait_ge(sV.h, sV.n)
                    return nc

            bb = contextlib.ExitStack()
            EBB = bb.enter_context
            pTf = EBB(nc.sbuf_tensor("pTf", [128, 2048], BF16))
            sq2k = EBB(nc.sbuf_tensor("sq2k", [64, 2, S], BF16))
            sq2q = EBB(nc.sbuf_tensor("sq2q", [64, 2, OWN], BF16))
            colmax = EBB(nc.sbuf_tensor("colmax", [64, 2], BF16))
            mk2p = EBB(nc.sbuf_tensor("mk2p", [1, 2], F32))
            r_row = EBB(nc.sbuf_tensor("r_row", [1, 512], F32))
            atnU = EBB(nc.sbuf_tensor("atnU", [64, 512], BF16))
            atn_st = EBB(nc.sbuf_tensor("atn_st", [64, 2, OWN], BF16))

            NPASS = NH * 4
            NSTEP = NPASS * NT
            ev_qk = [0] * NSTEP
            ev_expp = [0] * (NSTEP // 2)
            ev_av = [0] * NSTEP
            ev_epi = [0] * NPASS
            ev_spill = [0] * NH
            ev_kev = {}
            ev_qev = {}
            ev_row_h = [0] * NH
            ev_nq_h = [0] * NH
            ev_q2g = [0] * (NH // 2)
            ev_cm_h = [0] * (NH // 2)

            DVE.wait_ge(dG.h, ev_wA)
            for p8 in range(8):
                DVE.drain()
                DVE.tensor_copy(fcv_pack[:, p8, 0:64],
                                fcv_aug[:, p8 * 128:p8 * 128 + 64])
                DVE.drain()
                ins = DVE.tensor_copy(fcv_pack[:, p8, 65:129],
                                      fcv_aug[:, p8 * 128 + 64:p8 * 128 + 128])
            ev_fvp = sV.inc(ins)

            def prologue(p):
                h = 2 * p
                s0 = p * 8 * NT
                if p == 0:
                    PE.wait_ge(dG.h, ev_wA)
                    PE.wait_ge(sA.h, last_t1ev)
                    PE.wait_ge(sV.h, last_A_evac)
                else:
                    PE.wait_ge(sA.h, ev_expp[(s0 - 2) // 2])
                    ACT.wait_ge(sG.h, ev_q2g[p - 1])
                    DVE.wait_ge(sG.h, ev_q2g[p - 1])
                hc2 = slice(h * DK, (h + 2) * DK)
                for j in range(16):
                    if j >= 4:
                        if (j - 4) % 4 < 2:
                            PE.wait_ge(sA.h, ev_kev[(h, j - 4)])
                        else:
                            PE.wait_ge(sV.h, ev_kev[(h, j - 4)])
                    ins = PE.matmul(psA[:, (j % 4) * 512:(j % 4) * 512 + 512],
                                    fck_aug[:, hc2],
                                    t1kT[:, j * 512:(j + 1) * 512],
                                    start=True, stop=True)
                    evk = sP.inc(ins)
                    eng = ACT if (j % 4) < 2 else DVE
                    eng.wait_ge(sP.h, evk)
                    if eng is DVE:
                        eng.drain()
                    for pj in range(2):
                        srck = psA[pj * 64:(pj + 1) * 64,
                                   (j % 4) * 512:(j % 4) * 512 + 512]
                        dstk = kT_2[0:64, pj, j * 512:(j + 1) * 512]
                        if eng is ACT:
                            ins2 = eng.activation(dstk, srck, AF.Copy)
                        else:
                            ins2 = eng.tensor_copy(dstk, srck)
                    ev_kev[(h, j)] = (sA if eng is ACT else sV).inc(ins2)
                for jq in range(4):
                    prev = ev_kev[(h, 12 + jq)]
                    if (12 + jq) % 4 < 2:
                        PE.wait_ge(sA.h, prev)
                    else:
                        PE.wait_ge(sV.h, prev)
                    ins = PE.matmul(psA[:, jq * 512:jq * 512 + 512],
                                    fcq_aug[:, hc2],
                                    t1qT[:, jq * 512:(jq + 1) * 512],
                                    start=True, stop=True)
                    evq = sP.inc(ins)
                    eng = ACT if (jq % 4) < 2 else DVE
                    eng.wait_ge(sP.h, evq)
                    if eng is DVE:
                        eng.drain()
                    for parq in range(2):
                        srcq = psA[parq * 64:(parq + 1) * 64,
                                   jq * 512:jq * 512 + 512]
                        dstq = qT_2[0:64, parq, jq * 512:(jq + 1) * 512]
                        if eng is ACT:
                            ins = eng.activation(dstq, srcq, AF.Copy)
                        else:
                            ins = eng.tensor_copy(dstq, srcq)
                    ev_qev[(h, jq)] = (sA if eng is ACT else sV).inc(ins)

                ev_vev = [0] * NT
                for kt in range(NT):
                    b = kt % 4
                    if kt == 0:
                        PE.wait_ge(sA.h, ev_qev[(h, 1)])
                        PE.wait_ge(sV.h, ev_qev[(h, 3)])
                        if p == 0:
                            PE.wait_ge(sV.h, ev_fvp)
                            PE.wait_ge(sG.h, ev_fvp0)
                    elif kt >= 4:
                        PE.wait_ge(sV.h, ev_vev[kt - 4])
                    ins = PE.matmul(psA[:, b * 512:b * 512 + 130],
                                    t1vT[:, kt * 128:(kt + 1) * 128],
                                    fcv_pack[:, p, :], start=True, stop=True)
                    evv = sP.inc(ins)
                    DVE.wait_ge(sP.h, evv)
                    DVE.drain()
                    ev_vev[kt] = sV.inc(DVE.tensor_copy(
                        V_pair[:, kt, :], psA[:, b * 512:b * 512 + 130]))

                GP.wait_ge(sA.h, ev_kev[(h, 13)])
                GP.wait_ge(sV.h, ev_kev[(h, 15)])
                GP.wait_ge(sA.h, ev_qev[(h, 1)])
                GP.wait_ge(sV.h, ev_qev[(h, 3)])
                if p > 0:
                    GP.wait_ge(sP.h, ev_nq_h[h - 1])
                    GP.wait_ge(sV.h, ev_cm_h[p - 1])
                    DVE.wait_ge(sP.h, ev_nq_h[h - 1])
                evsq = [0, 0]
                for hh in range(2):
                    GP.drain()
                    evsq[hh] = sG.inc(GP.tensor_tensor(
                        out=sq2k[:, hh, :], in0=kT_2[0:64, hh, :],
                        in1=kT_2[0:64, hh, :], op=ALU.mult))
                for hh in range(2):
                    GP.drain()
                    ins = GP.tensor_tensor(out=sq2q[:, hh, :],
                                           in0=qT_2[0:64, hh, :],
                                           in1=qT_2[0:64, hh, :], op=ALU.mult)
                ev_q2g[p] = sG.inc(ins)
                for hh in range(2):
                    DVE.wait_ge(sG.h, evsq[hh])
                    DVE.drain()
                    ins = DVE.reduce_max(colmax[:, hh:hh + 1], sq2k[:, hh, :],
                                         axis=AX.X)
                ev_cm_h[p] = sV.inc(ins)

                PE.wait_ge(sV.h, ev_cm_h[p])
                ins = PE.matmul(psA[0:1, 384:386], ones64b[:, :], colmax[:, :],
                                start=True, stop=True)
                ev_mk = sP.inc(ins)
                DVE.wait_ge(sP.h, ev_mk)
                DVE.drain()
                ev_mkc = sV.inc(DVE.tensor_copy(mk2p[:, :], psA[0:1, 384:386]))
                for hh in range(2):
                    PE.wait_ge(sG.h, ev_q2g[p])
                    if hh == 0:
                        PE.wait_ge(sV.h, ev_mkc)
                    else:
                        PE.wait_ge(sA.h, ev_row_h[h])
                    for jq in range(4):
                        ins = PE.matmul(psA[0:1, jq * 512:jq * 512 + 512],
                                        ones64b[:, :],
                                        sq2q[:, hh, jq * 512:(jq + 1) * 512],
                                        start=True, stop=True)
                    ev_nq_h[h + hh] = sP.inc(ins)
                    ACT.wait_ge(sP.h, ev_nq_h[h + hh])
                    if hh == 0:
                        ACT.wait_ge(sV.h, ev_mkc)
                    ev_row_h[h + hh] = sA.inc(ACT.activation(
                        qT_2[64:65, hh, :], psA[0:1, 0:2048],
                        AF.Sqrt, scale=mk2p[:, hh:hh + 1]))

            def emit_qk(s):
                pss, kt = divmod(s, NT)
                head, qp = divmod(pss, 4)
                par = head % 2
                b = s % 4
                if s >= 4:
                    PE.wait_ge(sA.h, ev_expp[(s - 4) // 2])
                if s % (8 * NT) < 4:
                    PE.wait_ge(sA.h, ev_row_h[(s // (8 * NT)) * 2 + 1])
                ins = PE.matmul(psA[:, b * 512:(b + 1) * 512],
                                kT_2[:, par, kt * 128:(kt + 1) * 128],
                                qT_2[:, par, qp * 512:(qp + 1) * 512],
                                start=True, stop=True)
                ev_qk[s] = sP.inc(ins)

            def emit_expp(j):
                b = (2 * j) % 4
                ACT.wait_ge(sP.h, ev_qk[2 * j + 1])
                ev_expp[j] = sA.inc(ACT.activation(
                    pTf[:, b * 512:b * 512 + 1024],
                    psA[:, b * 512:b * 512 + 1024], AF.Exp, scale=0.125))

            def emit_av(s):
                pss, kt = divmod(s, NT)
                head = pss // 4
                hh = head % 2
                b = pss % 4
                PE.wait_ge(sA.h, ev_expp[s // 2])
                if kt == 0 and pss >= 4:
                    PE.wait_ge(sV.h, ev_epi[pss - 4])
                ins = PE.matmul(psB[0:65, b * 512:(b + 1) * 512],
                                V_pair[:, kt, hh * 65:hh * 65 + 65],
                                pTf[:, (s % 4) * 512:(s % 4) * 512 + 512],
                                start=(kt == 0), stop=(kt == NT - 1))
                ev_av[s] = sP.inc(ins)

            def emit_epi(pss):
                head, qp = divmod(pss, 4)
                par = head % 2
                b = pss % 4
                DVE.wait_ge(sP.h, ev_av[pss * NT + NT - 1])
                if head >= 2 and qp == 0:
                    DVE.wait_ge(dO.h, ev_spill[head - 2])
                DVE.drain()
                DVE.tensor_copy(atnU[:, :], psB[0:64, b * 512:(b + 1) * 512])
                DVE.drain()
                ev_rec = sV.inc(DVE.reciprocal(
                    r_row[:, :], psB[64:65, b * 512:(b + 1) * 512]))
                PE.wait_ge(sV.h, ev_rec)
                ins = PE.matmul(psB[0:64, b * 512:(b + 1) * 512],
                                ones1f[:, 0:64], r_row[:, :],
                                start=True, stop=True)
                ev_rbc = sP.inc(ins)
                DVE.wait_ge(sP.h, ev_rbc)
                DVE.drain()
                ev_epi[pss] = sV.inc(DVE.tensor_tensor(
                    out=atn_st[:, par, qp * 512:(qp + 1) * 512],
                    in0=atnU[:, :], in1=psB[0:64, b * 512:(b + 1) * 512],
                    op=ALU.mult))
                if qp == 3:
                    SP.wait_ge(sV.h, ev_epi[pss])
                    ev_spill[head] = dO.dma(
                        SP, attnT_hbm[head * DK:(head + 1) * DK, :],
                        atn_st[:, par, :])

            for s in range(NSTEP + 2):
                if s < NSTEP and s % (8 * NT) == 0:
                    prologue(s // (8 * NT))
                if s < NSTEP:
                    emit_qk(s)
                if s >= 2:
                    emit_av(s - 2)
                if s < NSTEP and s % 2 == 1:
                    emit_expp((s - 1) // 2)
                if s >= 2 and (s - 2) % NT == NT - 1:
                    emit_epi((s - 2) // NT)

            bb.close()
        snapB_P, snapB_A, snapB_V = sP.n, sA.n, sV.n

        with contextlib.ExitStack() as cc:
            EC = cc.enter_context
            out_acc = EC(nc.sbuf_tensor("out_acc", [128, NOT, D], F32))
            w1buf = EC(nc.sbuf_tensor("w1buf", [128, NCH, HF], BF16))
            w2buf = EC(nc.sbuf_tensor("w2buf", [128, NHC, D], BF16))
            hT = EC(nc.sbuf_tensor("hT", [128, NHC, 512], BF16))
            aT_ld = EC(nc.sbuf_tensor("aT_ld", [128, 2, NCH, 128], BF16))
            b2rows = EC(nc.sbuf_tensor("b2rows", [1, NE * D], BF16))
            fco_bb = EC(nc.sbuf_tensor("fco_bb", [128, D], F32))
            fco_brow = EC(nc.sbuf_tensor("fco_brow", [1, D], F32))
            fco_view = w2buf[:, 0:NCH, :]

            SP.wait_ge(dO.h, ev_spill[NH - 1])
            SP.wait_ge(sP.h, snapB_P)
            SP.wait_ge(sV.h, snapB_V)
            dS.dma(SP, out_acc[:, :, :], xp[:, 0:NOT, :])
            ev_xinit = dS.dma(SP, fco_brow[:, :], fco_b[None, :])
            GP.wait_ge(sP.h, snapB_P)
            GP.wait_ge(sV.h, snapB_V)
            GP.wait_ge(dO.h, ev_spill[NH - 1])
            dG.dma(GP, fco_view[:, :, :], fco_w.rearrange("(c p) e -> p c e", p=128))
            dG.dma(GP, b2rows[:, :], b2.rearrange("e d -> (e d)")[None, :])
            ev_fco = dG.n
            ev_b2r = dG.n
            PE.wait_ge(dS.h, ev_xinit)
            PE.wait_ge(sA.h, snapB_A)
            PE.wait_ge(sV.h, snapB_V)
            for half in range(2):
                ins = PE.matmul(psA[:, half * 512:(half + 1) * 512],
                                ones1f[:, :], fco_brow[:, half * 512:(half + 1) * 512],
                                start=True, stop=True)
            ev_bbmm = sP.inc(ins)
            ACT.wait_ge(sP.h, ev_bbmm)
            ev_bbe = sA.inc(ACT.activation(fco_bb[:, :], psA[:, 0:1024], AF.Copy))
            ins = PE.transpose(psA[:, 1024:1152], b1_raw[:, :], ident[:, :])
            ev_b1t = sP.inc(ins)
            ACT.wait_ge(sP.h, ev_b1t)
            ev_b1e = sA.inc(ACT.activation(b1T[:, :], psA[:, 1024:1152], AF.Copy))
            DVE.wait_ge(sA.h, ev_bbe)
            DVE.wait_ge(dS.h, ev_xinit)
            DVE.drain()
            for gt in range(NOT):
                ins = DVE.tensor_tensor(out=out_acc[:, gt, :], in0=out_acc[:, gt, :],
                                        in1=fco_bb[:, :], op=ALU.add)
            ev_init = sV.inc(ins)

            aTs = attnT_hbm.rearrange("(c p) t -> p c t", p=128)
            ev_c0mm = [0] * NOT
            ev_c0ev = [0] * NOT
            ev_ald = [0] * NOT
            for gt in range(NOT):
                if gt >= 2:
                    SP.wait_ge(sP.h, ev_c0mm[gt - 2])
                ev_ald[gt] = dS.dma(SP, aT_ld[:, gt % 2, :, :],
                                    aTs[:, :, gt * 128:(gt + 1) * 128])
                PE.wait_ge(dS.h, ev_ald[gt])
                if gt == 0:
                    PE.wait_ge(dG.h, ev_fco)
                if gt >= 2:
                    PE.wait_ge(sV.h, ev_c0ev[gt - 2])
                for half in range(2):
                    ps = psB[:, (gt % 2) * 1024 + half * 512:
                             (gt % 2) * 1024 + (half + 1) * 512]
                    for c in range(NCH):
                        ins = PE.matmul(ps, aT_ld[:, gt % 2, c, :],
                                        fco_view[:, c, half * 512:(half + 1) * 512],
                                        start=(c == 0), stop=(c == NCH - 1))
                ev_c0mm[gt] = sP.inc(ins)
                DVE.wait_ge(sP.h, ev_c0mm[gt])
                if gt == 0:
                    DVE.wait_ge(sV.h, ev_init)
                DVE.drain()
                for half in range(2):
                    ps = psB[:, (gt % 2) * 1024 + half * 512:
                             (gt % 2) * 1024 + (half + 1) * 512]
                    ins = DVE.tensor_tensor(
                        out=out_acc[:, gt, half * 512:(half + 1) * 512],
                        in0=ps, in1=out_acc[:, gt, half * 512:(half + 1) * 512],
                        op=ALU.add)
                ev_c0ev[gt] = sV.inc(ins)

            if phases == 3:
                SP.wait_ge(sV.h, ev_c0ev[NOT - 1])
                dO.dma(SP, out.rearrange("(t p) f -> p t f", p=128), out_acc[:, :, :])
                SP.wait_ge(dO.h, dO.n)
                return nc

            ev_we = [0] * NE
            ev_w1 = [0] * NE
            ev_hmm_last = [0] * NE
            ev_hev = {}
            ev_ymm = {}
            ev_yev = {}
            last_yev = 0
            for e in range(NE):
                if e == 0:
                    GP.wait_ge(sP.h, ev_c0mm[NOT - 1])
                else:
                    GP.wait_ge(sP.h, ev_hmm_last[e - 1])
                ev_w1[e] = dG.dma(GP, w1buf[:, :, :],
                                  w1[e].rearrange("(c p) h -> p c h", p=128))
                if e > 0:
                    GP.wait_ge(sP.h, ev_ymm[(e - 1, 3)])
                ev_we[e] = dG.dma(GP, w2buf[:, :, :],
                                  w2[e].rearrange("(c p) d -> p c d", p=128))

                for tt in range(4):
                    for Hc in range(NHC):
                        ps = psA[:, (Hc % 4) * 512:(Hc % 4) * 512 + 512]
                        if Hc == 0 and tt == 0:
                            PE.wait_ge(dG.h, ev_w1[e])
                            if e == 0:
                                PE.wait_ge(sA.h, ev_b1e)
                        if Hc >= 4:
                            PE.wait_ge(sA.h, ev_hev[(e, tt, Hc - 4)])
                        elif tt > 0:
                            PE.wait_ge(sA.h, ev_hev[(e, tt - 1, NHC - 4 + Hc)])
                        elif e > 0:
                            PE.wait_ge(sA.h, ev_hev[(e - 1, 3, NHC - 4 + Hc)])
                        for d in range(NCH):
                            ins = PE.matmul(ps, w1buf[:, d, Hc * 128:(Hc + 1) * 128],
                                            xnT_own[:, d, tt * 512:(tt + 1) * 512],
                                            start=(d == 0), stop=(d == NCH - 1))
                        evh = sP.inc(ins)
                        ev_hmm_last[e] = evh
                        ACT.wait_ge(sP.h, evh)
                        if tt > 0:
                            ACT.wait_ge(sP.h, ev_ymm[(e, tt - 1)])
                        elif e > 0:
                            ACT.wait_ge(sP.h, ev_ymm[(e - 1, 3)])
                        ev_hev[(e, tt, Hc)] = sA.inc(ACT.activation(
                            hT[:, Hc, :], ps, AF.Relu,
                            bias=b1T[:, e * NHC + Hc: e * NHC + Hc + 1]))
                    for ttok in range(4):
                        gt = tt * 4 + ttok
                        for fc in range(2):
                            bslot = (ttok * 2 + fc) % 4
                            ps = psB[:, bslot * 512:(bslot + 1) * 512]
                            PE.wait_ge(sA.h, ev_hev[(e, tt, NHC - 1)])
                            if ttok >= 2:
                                pk = ev_yev[(e, tt, ttok - 2, fc)]
                            elif tt > 0:
                                pk = ev_yev[(e, tt - 1, ttok + 2, fc)]
                            elif e > 0:
                                pk = ev_yev[(e - 1, 3, ttok + 2, fc)]
                            else:
                                pk = ev_c0ev[NOT - 1]
                            PE.wait_ge(sV.h, pk)
                            if tt == 0 and ttok == 0 and fc == 0:
                                PE.wait_ge(dG.h, ev_we[e])
                                if e == 0:
                                    PE.wait_ge(dG.h, ev_b2r)
                            PE.matmul(ps, ones1b[:, :],
                                      b2rows[:, e * D + fc * 512: e * D + (fc + 1) * 512],
                                      start=True, stop=False)
                            for Hc in range(NHC):
                                ins = PE.matmul(ps, hT[:, Hc, ttok * 128:(ttok + 1) * 128],
                                                w2buf[:, Hc, fc * 512:(fc + 1) * 512],
                                                start=False, stop=(Hc == NHC - 1))
                            ev_ymm[(e, tt)] = sP.inc(ins)
                            DVE.wait_ge(sP.h, ev_ymm[(e, tt)])
                            DVE.drain()
                            ev_yev[(e, tt, ttok, fc)] = sV.inc(DVE.scalar_tensor_tensor(
                                out=out_acc[:, gt, fc * 512:(fc + 1) * 512],
                                in0=ps, scalar=W_sb[:, gt, e:e + 1],
                                in1=out_acc[:, gt, fc * 512:(fc + 1) * 512],
                                op0=ALU.mult, op1=ALU.add))
                            last_yev = ev_yev[(e, tt, ttok, fc)]

            SP.wait_ge(sV.h, last_yev)
            dO.dma(SP, out.rearrange("(t p) f -> p t f", p=128), out_acc[:, :, :])
            SP.wait_ge(dO.h, dO.n)

    return nc


def kernel(**inputs):
    x_full = np.ascontiguousarray(inputs["L0_attn_z"])
    keymap = {
        "wq": "L0_attn_wq", "wk": "L0_attn_wk", "wv": "L0_attn_wv",
        "fcq_w": "L0_attn_fcq_w", "fcq_b": "L0_attn_fcq_b",
        "fck_w": "L0_attn_fck_w", "fck_b": "L0_attn_fck_b",
        "fcv_w": "L0_attn_fcv_w", "fcv_b": "L0_attn_fcv_b",
        "fco_w": "L0_attn_fco_w", "fco_b": "L0_attn_fco_b",
        "w_router": "L0_ffn_w_router", "b_router": "L0_ffn_b_router",
        "w1": "L0_ffn_w1", "b1": "L0_ffn_b1", "w2": "L0_ffn_w2", "b2": "L0_ffn_b2",
    }
    weights = {k: np.ascontiguousarray(inputs[v]) for k, v in keymap.items()}

    nc = build_nc()
    in_maps = []
    for core in range(NC):
        b, q = core // 4, core % 4
        xb = x_full[b]
        own = xb[q * OWN:(q + 1) * OWN]
        rest = np.concatenate([xb[:q * OWN], xb[(q + 1) * OWN:]], axis=0)
        xc = np.ascontiguousarray(np.concatenate([own, rest], axis=0))
        m = {"x": xc}
        m.update(weights)
        in_maps.append(m)

    res = run_bass_kernel_spmd(nc, in_maps, core_ids=list(range(NC)))
    outp = np.empty((B, S, D), dtype=np.float32)
    for core in range(NC):
        b, q = core // 4, core % 4
        outp[b, q * OWN:(q + 1) * OWN] = res.results[core]["out"]
    return outp


# version 15
# speedup vs baseline: 1.5232x; 1.5232x over previous
import contextlib
import numpy as np

import concourse.bass as bass
import concourse.mybir as mybir
from concourse.bass_utils import run_bass_kernel_spmd

F32 = mybir.dt.float32
BF16 = mybir.dt.bfloat16
AF = mybir.ActivationFunctionType
ALU = mybir.AluOpType
AX = mybir.AxisListType

B, S, D = 2, 8192, 1024
NH, DK = 16, 64
NE, HF = 8, 2048
NC = 8
OWN = 2048
NT = S // 128
NOT = OWN // 128
NCH = D // 128
NHC = HF // 128
NG = NT // 4


class Sem:
    def __init__(self, ctx, nc, name):
        self.h = ctx.enter_context(nc.semaphore(name))
        self.n = 0

    def inc(self, instr, k=1):
        instr.then_inc(self.h, k)
        self.n += k
        return self.n

    def dma(self, eng, dst, src):
        eng.wait_ge(self.h, self.n)
        ins = eng.dma_start(dst, src)
        return self.inc(ins, 16)


def build_nc(phases=4):
    nc = bass.Bass("TRN2", target_bir_lowering=False)

    x = nc.dram_tensor("x", [S, D], F32, kind="ExternalInput")
    wq = nc.dram_tensor("wq", [D, DK], F32, kind="ExternalInput")
    wk = nc.dram_tensor("wk", [D, DK], F32, kind="ExternalInput")
    wv = nc.dram_tensor("wv", [D, DK], F32, kind="ExternalInput")
    fcq_w = nc.dram_tensor("fcq_w", [DK, D], F32, kind="ExternalInput")
    fck_w = nc.dram_tensor("fck_w", [DK, D], F32, kind="ExternalInput")
    fcv_w = nc.dram_tensor("fcv_w", [DK, D], F32, kind="ExternalInput")
    fcq_b = nc.dram_tensor("fcq_b", [D], F32, kind="ExternalInput")
    fck_b = nc.dram_tensor("fck_b", [D], F32, kind="ExternalInput")
    fcv_b = nc.dram_tensor("fcv_b", [D], F32, kind="ExternalInput")
    fco_w = nc.dram_tensor("fco_w", [D, D], F32, kind="ExternalInput")
    fco_b = nc.dram_tensor("fco_b", [D], F32, kind="ExternalInput")
    w_router = nc.dram_tensor("w_router", [D, NE], F32, kind="ExternalInput")
    b_router = nc.dram_tensor("b_router", [NE], F32, kind="ExternalInput")
    w1 = nc.dram_tensor("w1", [NE, D, HF], F32, kind="ExternalInput")
    b1 = nc.dram_tensor("b1", [NE, HF], F32, kind="ExternalInput")
    w2 = nc.dram_tensor("w2", [NE, HF, D], F32, kind="ExternalInput")
    b2 = nc.dram_tensor("b2", [NE, D], F32, kind="ExternalInput")
    out = nc.dram_tensor("out", [OWN, D], F32, kind="ExternalOutput")
    attnT_hbm = nc.dram_tensor("attnT_hbm", [D, OWN], BF16, kind="Internal")

    xp = x.rearrange("(n p) f -> p n f", p=128)

    PE, ACT, DVE, SP, GP = nc.tensor, nc.scalar, nc.vector, nc.sync, nc.gpsimd

    with contextlib.ExitStack() as ctx:
        E = ctx.enter_context

        ident = E(nc.sbuf_tensor("ident", [128, 128], F32))
        ones1b = E(nc.sbuf_tensor("ones1b", [1, 128], BF16))
        ones1f = E(nc.sbuf_tensor("ones1f", [1, 128], F32))
        ones64b = E(nc.sbuf_tensor("ones64b", [64, 1], BF16))
        eps_t = E(nc.sbuf_tensor("eps_t", [128, 1], F32))
        xnT_own = E(nc.sbuf_tensor("xnT_own", [128, NCH, OWN], BF16))
        W_sb = E(nc.sbuf_tensor("W_sb", [128, NOT, NE], F32))
        b1T = E(nc.sbuf_tensor("b1T", [128, 128], F32))
        b1_raw = E(nc.sbuf_tensor("b1_raw", [128, 128], F32))
        brout_row = E(nc.sbuf_tensor("brout_row", [1, NE], F32))

        psA = E(nc.psum_tensor("psA", [128, 2048], F32))
        psB = E(nc.psum_tensor("psB", [128, 2048], F32))

        dS = Sem(ctx, nc, "dS")
        dG = Sem(ctx, nc, "dG")
        dO = Sem(ctx, nc, "dO")
        sP = Sem(ctx, nc, "sP")
        sA = Sem(ctx, nc, "sA")
        sV = Sem(ctx, nc, "sV")
        sG = Sem(ctx, nc, "sG")

        blk = E(nc.Block())

        GP.memset(ident[:, :], 0.0)
        GP.drain()
        GP.affine_select(out=ident[:, :], in_=ident[:, :],
                         compare_op=ALU.not_equal, fill=1.0, base=0,
                         pattern=[[-1, 128]], channel_multiplier=1)
        GP.memset(ones1b[:, :], 1.0)
        GP.memset(ones1f[:, :], 1.0)
        GP.memset(ones64b[:, :], 1.0)
        GP.memset(eps_t[:, :], 1e-5)

        dS.dma(SP, brout_row[:, :], b_router[None, :])
        dS.dma(SP, b1_raw[:, :], b1.rearrange("e (q a) -> (e q) a", a=128))
        ev_small = dS.n

        with contextlib.ExitStack() as ab:
            EA = ab.enter_context
            t1kT = EA(nc.sbuf_tensor("t1kT", [128, S], BF16))
            t1vT = EA(nc.sbuf_tensor("t1vT", [128, S], BF16))
            t1qT = EA(nc.sbuf_tensor("t1qT", [128, OWN], BF16))
            fcq_aug = EA(nc.sbuf_tensor("fcq_aug", [128, D], BF16))
            fck_aug = EA(nc.sbuf_tensor("fck_aug", [128, D], BF16))
            fcv_aug = EA(nc.sbuf_tensor("fcv_aug", [128, D], BF16))
            GP.memset(t1kT[64:128, :], 0.0)
            GP.memset(t1vT[64:128, :], 0.0)
            GP.memset(t1qT[64:128, :], 0.0)
            GP.memset(fcq_aug[64:128, :], 0.0)
            GP.memset(fck_aug[64:128, :], 0.0)
            GP.memset(fcv_aug[64:128, :], 0.0)
            kT_2 = EA(nc.sbuf_tensor("kT_2", [128, 2, S], BF16))
            qT_2 = EA(nc.sbuf_tensor("qT_2", [128, 2, OWN], BF16))
            V_pair = EA(nc.sbuf_tensor("V_pair", [128, NT, 130], BF16))
            fcv_pack = EA(nc.sbuf_tensor("fcv_pack", [128, 8, 130], BF16))
            GP.memset(kT_2[64:128, :, :], 0.0)
            GP.memset(qT_2[64:128, :, :], 0.0)
            GP.drain()
            GP.memset(t1kT[64:65, :], 1.0)
            GP.memset(t1vT[64:65, :], 1.0)
            GP.memset(kT_2[64:65, :, :], -1.0)
            ev_consts = sG.inc(GP.memset(t1qT[64:65, :], 1.0))

            with contextlib.ExitStack() as aa:
                EB = aa.enter_context
                wqkv_bf = EB(nc.sbuf_tensor("wqkv_bf", [128, NCH, 4 * DK], BF16))
                w_rout_f = EB(nc.sbuf_tensor("w_rout_f", [128, NCH, NE], F32))
                xg = EB(nc.sbuf_tensor("xg", [128, 2, D], F32))
                xn_f = EB(nc.sbuf_tensor("xn_f", [128, 2, D], F32))
                xnT_st = EB(nc.sbuf_tensor("xnT_st", [128, 2, NCH, 512], BF16))
                xnTf_st = EB(nc.sbuf_tensor("xnTf_st", [128, 2, NCH, 128], F32))
                mvst = EB(nc.sbuf_tensor("mvst", [128, 12], F32))
                lnt = EB(nc.sbuf_tensor("lnt", [128, 2, 6], F32))
                xc_f = EB(nc.sbuf_tensor("xc_f", [128, 2, D], F32))
                rl_sc = EB(nc.sbuf_tensor("rl_sc", [128, 8, NE], F32))
                rst = EB(nc.sbuf_tensor("rst", [128, 4], F32))

                GP.memset(wqkv_bf[:, :, DK:2 * DK], 0.0)
                GP.memset(fcv_pack[:, :, 64:65], 0.0)
                GP.memset(fcv_pack[:, :, 129:130], 0.0)
                GP.drain()
                GP.memset(fcv_pack[64:65, :, 64:65], 1.0)
                ev_fvp0 = sG.inc(GP.memset(fcv_pack[64:65, :, 129:130], 1.0))
                dG.dma(GP, wqkv_bf[:, :, 0:DK], wq.rearrange("(c p) k -> p c k", p=128))
                dG.dma(GP, wqkv_bf[:, :, 2 * DK:3 * DK], wk.rearrange("(c p) k -> p c k", p=128))
                dG.dma(GP, wqkv_bf[:, :, 3 * DK:4 * DK], wv.rearrange("(c p) k -> p c k", p=128))
                ev_wqkv = dG.n
                for wsrc, bsrc, dst in ((fcq_w, fcq_b, fcq_aug), (fck_w, fck_b, fck_aug),
                                        (fcv_w, fcv_b, fcv_aug)):
                    dG.dma(GP, dst[0:64, :], wsrc[:, :])
                    dG.dma(GP, dst[64:65, :], bsrc[None, :])
                ev_wA = dG.n
                ev_wrout = dS.dma(SP, w_rout_f[:, :, :],
                                  w_router.rearrange("(c p) e -> p c e", p=128))

                ev_xdma = [0] * NT
                ev_xn = [0] * NT
                ev_sub = [0] * NT
                ev_tp = [0] * NT
                ev_evac = [0] * NT
                ev_evacf = [0] * NT
                ev_t1 = [0] * NG
                ev_t1ev = [0] * NG
                ev_rlog = [0] * NOT
                ev_rW = [0] * NOT

                def stage1(t):
                    own = t < NOT
                    if t == 0:
                        ACT.wait_ge(sG.h, ev_consts)
                    if t >= 2:
                        SP.wait_ge(sV.h, ev_sub[t - 2])
                    ev_xdma[t] = dS.dma(SP, xg[:, t % 2, :], xp[:, t, :])
                    xtile = xg[:, t % 2, :]
                    lb = lnt[:, t % 2, :]
                    DVE.wait_ge(dS.h, ev_xdma[t])
                    if t >= 2:
                        DVE.wait_ge(sA.h, ev_xn[t - 2])
                    if t > 0:
                        DVE.drain()
                    DVE.bn_stats(mvst[:, 0:6], xtile[:, 0:512])
                    DVE.bn_stats(mvst[:, 6:12], xtile[:, 512:1024])
                    DVE.drain()
                    ev1 = sV.inc(DVE.bn_aggr(lb[:, 0:2], mvst[:, 0:12]))
                    DVE.drain()
                    ev_sub[t] = sV.inc(DVE.tensor_scalar(
                        out=xc_f[:, t % 2, :], in0=xtile,
                        scalar1=lb[:, 0:1], scalar2=None, op0=ALU.subtract))
                    ACT.wait_ge(sV.h, ev1)
                    if t > 0:
                        ACT.drain()
                    ACT.activation(lb[:, 4:5], lb[:, 1:2], AF.Ln, bias=eps_t[:, :])
                    ACT.drain()
                    ACT.activation(lb[:, 2:3], lb[:, 4:5], AF.Exp, scale=-0.5)
                    ACT.drain()
                    ACT.wait_ge(sV.h, ev_sub[t])
                    ev_xn[t] = sA.inc(ACT.activation(xn_f[:, t % 2, :],
                                                     xc_f[:, t % 2, :],
                                                     AF.Identity, scale=lb[:, 2:3]))

                def stage2(t):
                    own = t < NOT
                    g4 = t // 4
                    if t == 0:
                        PE.wait_ge(sG.h, ev_consts)
                    PE.wait_ge(sA.h, ev_xn[t])
                    if t >= 2:
                        if t - 2 < NOT:
                            PE.wait_ge(sA.h, ev_evacf[t - 2])
                        else:
                            PE.wait_ge(sV.h, ev_evac[t - 2])
                    base = (t % 2) * 1024
                    for c in range(NCH):
                        off = base + (c // 4) * 512 + (c % 4) * 128
                        ins = PE.transpose(psA[:, off:off + 128],
                                           xn_f[:, t % 2, c * 128:(c + 1) * 128],
                                           ident[:, :])
                    ev_tp[t] = sP.inc(ins)
                    if own:
                        ACT.wait_ge(sP.h, ev_tp[t])
                        for j in range(2):
                            ins = ACT.activation(
                                xnTf_st[:, t % 2, j * 4:(j + 1) * 4, :],
                                psA[:, base + j * 512: base + j * 512 + 512], AF.Copy)
                        ev_evacf[t] = sA.inc(ins)
                        DVE.wait_ge(sA.h, ev_evacf[t])
                        ins = DVE.tensor_copy(
                            xnT_own[:, :, t * 128:(t + 1) * 128],
                            xnTf_st[:, t % 2, :, :])
                        ev_evac[t] = sV.inc(ins)
                    else:
                        DVE.wait_ge(sP.h, ev_tp[t])
                        if g4 >= 2:
                            DVE.wait_ge(sP.h, ev_t1[g4 - 2])
                        for j in range(2):
                            srcp = psA[:, base + j * 512: base + j * 512 + 512]
                            dst = xnT_st[:, g4 % 2, j * 4:(j + 1) * 4,
                                         (t % 4) * 128:(t % 4) * 128 + 128]
                            ins = DVE.tensor_copy(dst, srcp)
                        ev_evac[t] = sV.inc(ins)
                        ev_evacf[t] = 0
                    if t % 4 == 3:
                        PE.wait_ge(sV.h, ev_evac[t])
                        if g4 == 0:
                            PE.wait_ge(dG.h, ev_wqkv)
                        if g4 >= 1:
                            PE.wait_ge(sA.h, ev_t1ev[g4 - 1])
                        projs = [(2, psB[:, 0:512])]
                        if own:
                            projs.append((0, psB[:, 512:1024]))
                        for wi, ps in projs:
                            for c in range(NCH):
                                if own:
                                    rhs = xnT_own[:, c, g4 * 512:(g4 + 1) * 512]
                                else:
                                    rhs = xnT_st[:, g4 % 2, c, :]
                                ins = PE.matmul(ps, wqkv_bf[:, c, wi * DK:(wi + 2) * DK],
                                                rhs, start=(c == 0), stop=(c == NCH - 1))
                        ev_t1[g4] = sP.inc(ins)
                        ACT.wait_ge(sP.h, ev_t1[g4])
                        ins = ACT.activation(t1kT[0:64, g4 * 512:(g4 + 1) * 512],
                                             psB[0:64, 0:512], AF.Copy)
                        ins = ACT.activation(t1vT[0:64, g4 * 512:(g4 + 1) * 512],
                                             psB[64:128, 0:512], AF.Copy)
                        if own:
                            ins = ACT.activation(t1qT[0:64, g4 * 512:(g4 + 1) * 512],
                                                 psB[0:64, 512:1024], AF.Copy)
                        ev_t1ev[g4] = sA.inc(ins)

                def stage3(t):
                    slot = t % 4
                    rps = psB[:, 1536 + slot * NE: 1536 + (slot + 1) * NE]
                    PE.wait_ge(sA.h, ev_evacf[t])
                    if t == 0:
                        PE.wait_ge(dS.h, ev_wrout)
                    else:
                        PE.wait_ge(sV.h, ev_rlog[t - 1])
                    PE.matmul(rps, ones1f[:, :], brout_row[:, :], start=True, stop=False)
                    for c in range(NCH):
                        ins = PE.matmul(rps, xnTf_st[:, t % 2, c, :], w_rout_f[:, c, :],
                                        start=False, stop=(c == NCH - 1))
                    evr = sP.inc(ins)
                    DVE.wait_ge(sP.h, evr)
                    if t >= 4:
                        DVE.wait_ge(sV.h, ev_rW[t - 4])
                    DVE.drain()
                    DVE.tensor_copy(rl_sc[:, slot, :], rps)
                    DVE.drain()
                    DVE.reduce_max(rst[:, 0:1], rl_sc[:, slot, :], axis=AX.X)
                    DVE.drain()
                    ev_rlog[t] = sV.inc(DVE.tensor_scalar(
                        out=rst[:, 1:2], in0=rst[:, 0:1],
                        scalar1=-1.0, scalar2=None, op0=ALU.mult))
                    ACT.wait_ge(sV.h, ev_rlog[t])
                    ACT.drain()
                    ev5 = sA.inc(ACT.activation(rl_sc[:, 4 + slot, :], rl_sc[:, slot, :],
                                                AF.Exp, bias=rst[:, 1:2]))
                    DVE.wait_ge(sA.h, ev5)
                    DVE.drain()
                    DVE.reduce_sum(rst[:, 2:3], rl_sc[:, 4 + slot, :], axis=AX.X)
                    DVE.drain()
                    DVE.reciprocal(rst[:, 3:4], rst[:, 2:3])
                    DVE.drain()
                    ev_rW[t] = sV.inc(DVE.tensor_scalar(
                        out=W_sb[:, t, :], in0=rl_sc[:, slot, :],
                        scalar1=rst[:, 0:1], scalar2=rst[:, 3:4],
                        op0=ALU.is_equal, op1=ALU.mult))

                for t in range(NT + 2):
                    if t < NT:
                        stage1(t)
                    if 1 <= t <= NT:
                        stage2(t - 1)
                    if 2 <= t and t - 2 < NOT:
                        stage3(t - 2)

                last_t1ev = ev_t1ev[NG - 1]
                last_A_evac = ev_evac[NT - 1]

                if phases in (1, 9, 10, 11, 12, 13):
                    if phases in (9, 10):
                        SP.wait_ge(sA.h, sA.n)
                        SP.wait_ge(sV.h, sV.n)
                        return nc
                    SP.wait_ge(sA.h, sA.n)
                    SP.wait_ge(sV.h, sV.n)
                    return nc

            bb = contextlib.ExitStack()
            EBB = bb.enter_context
            pTf = EBB(nc.sbuf_tensor("pTf", [128, 2048], BF16))
            sq2k = EBB(nc.sbuf_tensor("sq2k", [64, 2, S], BF16))
            sq2q = EBB(nc.sbuf_tensor("sq2q", [64, 2, OWN], BF16))
            colmax = EBB(nc.sbuf_tensor("colmax", [64, 2], BF16))
            mk2p = EBB(nc.sbuf_tensor("mk2p", [1, 2], F32))
            r_row = EBB(nc.sbuf_tensor("r_row", [1, 512], F32))
            atnU = EBB(nc.sbuf_tensor("atnU", [64, 512], BF16))
            atn_st = EBB(nc.sbuf_tensor("atn_st", [64, 2, OWN], BF16))

            NPASS = NH * 4
            NSTEP = NPASS * NT
            ev_qk = [0] * NSTEP
            ev_expp = [0] * (NSTEP // 2)
            ev_av = [0] * NSTEP
            ev_epi = [0] * NPASS
            ev_spill = [0] * NH
            ev_kev = {}
            ev_qev = {}
            ev_row_h = [0] * NH
            ev_nq_h = [0] * NH
            ev_q2g = [0] * (NH // 2)
            ev_cm_h = [0] * (NH // 2)

            DVE.wait_ge(dG.h, ev_wA)
            for p8 in range(8):
                DVE.drain()
                DVE.tensor_copy(fcv_pack[:, p8, 0:64],
                                fcv_aug[:, p8 * 128:p8 * 128 + 64])
                DVE.drain()
                ins = DVE.tensor_copy(fcv_pack[:, p8, 65:129],
                                      fcv_aug[:, p8 * 128 + 64:p8 * 128 + 128])
            ev_fvp = sV.inc(ins)

            def prologue(p):
                h = 2 * p
                s0 = p * 8 * NT
                if p == 0:
                    PE.wait_ge(dG.h, ev_wA)
                    PE.wait_ge(sA.h, last_t1ev)
                    PE.wait_ge(sV.h, last_A_evac)
                else:
                    PE.wait_ge(sA.h, ev_expp[(s0 - 2) // 2])
                    ACT.wait_ge(sG.h, ev_q2g[p - 1])
                    DVE.wait_ge(sG.h, ev_q2g[p - 1])
                hc2 = slice(h * DK, (h + 2) * DK)
                for j in range(16):
                    if j >= 4:
                        if (j - 4) % 4 < 2:
                            PE.wait_ge(sA.h, ev_kev[(h, j - 4)])
                        else:
                            PE.wait_ge(sV.h, ev_kev[(h, j - 4)])
                    ins = PE.matmul(psA[:, (j % 4) * 512:(j % 4) * 512 + 512],
                                    fck_aug[:, hc2],
                                    t1kT[:, j * 512:(j + 1) * 512],
                                    start=True, stop=True)
                    evk = sP.inc(ins)
                    eng = ACT if (j % 4) < 2 else DVE
                    eng.wait_ge(sP.h, evk)
                    if eng is DVE:
                        eng.drain()
                    for pj in range(2):
                        srck = psA[pj * 64:(pj + 1) * 64,
                                   (j % 4) * 512:(j % 4) * 512 + 512]
                        dstk = kT_2[0:64, pj, j * 512:(j + 1) * 512]
                        if eng is ACT:
                            ins2 = eng.activation(dstk, srck, AF.Copy)
                        else:
                            ins2 = eng.tensor_copy(dstk, srck)
                    ev_kev[(h, j)] = (sA if eng is ACT else sV).inc(ins2)
                for jq in range(4):
                    prev = ev_kev[(h, 12 + jq)]
                    if (12 + jq) % 4 < 2:
                        PE.wait_ge(sA.h, prev)
                    else:
                        PE.wait_ge(sV.h, prev)
                    ins = PE.matmul(psA[:, jq * 512:jq * 512 + 512],
                                    fcq_aug[:, hc2],
                                    t1qT[:, jq * 512:(jq + 1) * 512],
                                    start=True, stop=True)
                    evq = sP.inc(ins)
                    eng = ACT if (jq % 4) < 2 else DVE
                    eng.wait_ge(sP.h, evq)
                    if eng is DVE:
                        eng.drain()
                    for parq in range(2):
                        srcq = psA[parq * 64:(parq + 1) * 64,
                                   jq * 512:jq * 512 + 512]
                        dstq = qT_2[0:64, parq, jq * 512:(jq + 1) * 512]
                        if eng is ACT:
                            ins = eng.activation(dstq, srcq, AF.Copy)
                        else:
                            ins = eng.tensor_copy(dstq, srcq)
                    ev_qev[(h, jq)] = (sA if eng is ACT else sV).inc(ins)

                ev_vev = [0] * NT
                for kt in range(NT):
                    b = kt % 4
                    if kt == 0:
                        PE.wait_ge(sA.h, ev_qev[(h, 1)])
                        PE.wait_ge(sV.h, ev_qev[(h, 3)])
                        if p == 0:
                            PE.wait_ge(sV.h, ev_fvp)
                            PE.wait_ge(sG.h, ev_fvp0)
                    elif kt >= 4:
                        PE.wait_ge(sV.h, ev_vev[kt - 4])
                    ins = PE.matmul(psA[:, b * 512:b * 512 + 130],
                                    t1vT[:, kt * 128:(kt + 1) * 128],
                                    fcv_pack[:, p, :], start=True, stop=True)
                    evv = sP.inc(ins)
                    DVE.wait_ge(sP.h, evv)
                    DVE.drain()
                    ev_vev[kt] = sV.inc(DVE.tensor_copy(
                        V_pair[:, kt, :], psA[:, b * 512:b * 512 + 130]))

                GP.wait_ge(sA.h, ev_kev[(h, 13)])
                GP.wait_ge(sV.h, ev_kev[(h, 15)])
                GP.wait_ge(sA.h, ev_qev[(h, 1)])
                GP.wait_ge(sV.h, ev_qev[(h, 3)])
                if p > 0:
                    GP.wait_ge(sP.h, ev_nq_h[h - 1])
                    GP.wait_ge(sV.h, ev_cm_h[p - 1])
                    DVE.wait_ge(sP.h, ev_nq_h[h - 1])
                evsq = [0, 0]
                for hh in range(2):
                    GP.drain()
                    evsq[hh] = sG.inc(GP.tensor_tensor(
                        out=sq2k[:, hh, :], in0=kT_2[0:64, hh, :],
                        in1=kT_2[0:64, hh, :], op=ALU.mult))
                for hh in range(2):
                    GP.drain()
                    ins = GP.tensor_tensor(out=sq2q[:, hh, :],
                                           in0=qT_2[0:64, hh, :],
                                           in1=qT_2[0:64, hh, :], op=ALU.mult)
                ev_q2g[p] = sG.inc(ins)
                for hh in range(2):
                    DVE.wait_ge(sG.h, evsq[hh])
                    DVE.drain()
                    ins = DVE.reduce_max(colmax[:, hh:hh + 1], sq2k[:, hh, :],
                                         axis=AX.X)
                ev_cm_h[p] = sV.inc(ins)

                PE.wait_ge(sV.h, ev_cm_h[p])
                ins = PE.matmul(psA[0:1, 384:386], ones64b[:, :], colmax[:, :],
                                start=True, stop=True)
                ev_mk = sP.inc(ins)
                DVE.wait_ge(sP.h, ev_mk)
                DVE.drain()
                ev_mkc = sV.inc(DVE.tensor_copy(mk2p[:, :], psA[0:1, 384:386]))
                for hh in range(2):
                    PE.wait_ge(sG.h, ev_q2g[p])
                    if hh == 0:
                        PE.wait_ge(sV.h, ev_mkc)
                    else:
                        PE.wait_ge(sA.h, ev_row_h[h])
                    for jq in range(4):
                        ins = PE.matmul(psA[0:1, jq * 512:jq * 512 + 512],
                                        ones64b[:, :],
                                        sq2q[:, hh, jq * 512:(jq + 1) * 512],
                                        start=True, stop=True)
                    ev_nq_h[h + hh] = sP.inc(ins)
                    ACT.wait_ge(sP.h, ev_nq_h[h + hh])
                    if hh == 0:
                        ACT.wait_ge(sV.h, ev_mkc)
                    ev_row_h[h + hh] = sA.inc(ACT.activation(
                        qT_2[64:65, hh, :], psA[0:1, 0:2048],
                        AF.Sqrt, scale=mk2p[:, hh:hh + 1]))

            def emit_qk(s):
                pss, kt = divmod(s, NT)
                head, qp = divmod(pss, 4)
                par = head % 2
                b = s % 4
                if s >= 4:
                    PE.wait_ge(sA.h, ev_expp[(s - 4) // 2])
                if s % (8 * NT) < 4:
                    PE.wait_ge(sA.h, ev_row_h[(s // (8 * NT)) * 2 + 1])
                ins = PE.matmul(psA[:, b * 512:(b + 1) * 512],
                                kT_2[:, par, kt * 128:(kt + 1) * 128],
                                qT_2[:, par, qp * 512:(qp + 1) * 512],
                                start=True, stop=True)
                ev_qk[s] = sP.inc(ins)

            def emit_expp(j):
                b = (2 * j) % 4
                ACT.wait_ge(sP.h, ev_qk[2 * j + 1])
                ev_expp[j] = sA.inc(ACT.activation(
                    pTf[:, b * 512:b * 512 + 1024],
                    psA[:, b * 512:b * 512 + 1024], AF.Exp, scale=0.125))

            def emit_av(s):
                pss, kt = divmod(s, NT)
                head = pss // 4
                hh = head % 2
                b = pss % 4
                PE.wait_ge(sA.h, ev_expp[s // 2])
                if kt == 0 and pss >= 4:
                    PE.wait_ge(sV.h, ev_epi[pss - 4])
                ins = PE.matmul(psB[0:65, b * 512:(b + 1) * 512],
                                V_pair[:, kt, hh * 65:hh * 65 + 65],
                                pTf[:, (s % 4) * 512:(s % 4) * 512 + 512],
                                start=(kt == 0), stop=(kt == NT - 1))
                ev_av[s] = sP.inc(ins)

            def emit_epi(pss):
                head, qp = divmod(pss, 4)
                par = head % 2
                b = pss % 4
                DVE.wait_ge(sP.h, ev_av[pss * NT + NT - 1])
                if head >= 2 and qp == 0:
                    DVE.wait_ge(dO.h, ev_spill[head - 2])
                DVE.drain()
                DVE.tensor_copy(atnU[:, :], psB[0:64, b * 512:(b + 1) * 512])
                DVE.drain()
                ev_rec = sV.inc(DVE.reciprocal(
                    r_row[:, :], psB[64:65, b * 512:(b + 1) * 512]))
                PE.wait_ge(sV.h, ev_rec)
                ins = PE.matmul(psB[0:64, b * 512:(b + 1) * 512],
                                ones1f[:, 0:64], r_row[:, :],
                                start=True, stop=True)
                ev_rbc = sP.inc(ins)
                DVE.wait_ge(sP.h, ev_rbc)
                DVE.drain()
                ev_epi[pss] = sV.inc(DVE.tensor_tensor(
                    out=atn_st[:, par, qp * 512:(qp + 1) * 512],
                    in0=atnU[:, :], in1=psB[0:64, b * 512:(b + 1) * 512],
                    op=ALU.mult))
                if qp == 3:
                    SP.wait_ge(sV.h, ev_epi[pss])
                    ev_spill[head] = dO.dma(
                        SP, attnT_hbm[head * DK:(head + 1) * DK, :],
                        atn_st[:, par, :])

            for s in range(NSTEP + 3):
                if s < NSTEP and s % (8 * NT) == 0:
                    prologue(s // (8 * NT))
                if s < NSTEP:
                    emit_qk(s)
                if s >= 3:
                    emit_av(s - 3)
                if s < NSTEP and s % 2 == 1:
                    emit_expp((s - 1) // 2)
                if s >= 3 and (s - 3) % NT == NT - 1:
                    emit_epi((s - 3) // NT)

            bb.close()
        snapB_P, snapB_A, snapB_V = sP.n, sA.n, sV.n

        with contextlib.ExitStack() as cc:
            EC = cc.enter_context
            out_acc = EC(nc.sbuf_tensor("out_acc", [128, NOT, D], F32))
            w1buf = EC(nc.sbuf_tensor("w1buf", [128, NCH, HF], BF16))
            w2buf = EC(nc.sbuf_tensor("w2buf", [128, NHC, D], BF16))
            hT = EC(nc.sbuf_tensor("hT", [128, NHC, 512], BF16))
            aT_ld = EC(nc.sbuf_tensor("aT_ld", [128, 2, NCH, 128], BF16))
            b2rows = EC(nc.sbuf_tensor("b2rows", [1, NE * D], BF16))
            fco_bb = EC(nc.sbuf_tensor("fco_bb", [128, D], F32))
            fco_brow = EC(nc.sbuf_tensor("fco_brow", [1, D], F32))
            fco_view = w2buf[:, 0:NCH, :]

            SP.wait_ge(dO.h, ev_spill[NH - 1])
            SP.wait_ge(sP.h, snapB_P)
            SP.wait_ge(sV.h, snapB_V)
            dS.dma(SP, out_acc[:, :, :], xp[:, 0:NOT, :])
            ev_xinit = dS.dma(SP, fco_brow[:, :], fco_b[None, :])
            GP.wait_ge(sP.h, snapB_P)
            GP.wait_ge(sV.h, snapB_V)
            GP.wait_ge(dO.h, ev_spill[NH - 1])
            dG.dma(GP, fco_view[:, :, :], fco_w.rearrange("(c p) e -> p c e", p=128))
            dG.dma(GP, b2rows[:, :], b2.rearrange("e d -> (e d)")[None, :])
            ev_fco = dG.n
            ev_b2r = dG.n
            PE.wait_ge(dS.h, ev_xinit)
            PE.wait_ge(sA.h, snapB_A)
            PE.wait_ge(sV.h, snapB_V)
            for half in range(2):
                ins = PE.matmul(psA[:, half * 512:(half + 1) * 512],
                                ones1f[:, :], fco_brow[:, half * 512:(half + 1) * 512],
                                start=True, stop=True)
            ev_bbmm = sP.inc(ins)
            ACT.wait_ge(sP.h, ev_bbmm)
            ev_bbe = sA.inc(ACT.activation(fco_bb[:, :], psA[:, 0:1024], AF.Copy))
            ins = PE.transpose(psA[:, 1024:1152], b1_raw[:, :], ident[:, :])
            ev_b1t = sP.inc(ins)
            ACT.wait_ge(sP.h, ev_b1t)
            ev_b1e = sA.inc(ACT.activation(b1T[:, :], psA[:, 1024:1152], AF.Copy))
            DVE.wait_ge(sA.h, ev_bbe)
            DVE.wait_ge(dS.h, ev_xinit)
            DVE.drain()
            for gt in range(NOT):
                ins = DVE.tensor_tensor(out=out_acc[:, gt, :], in0=out_acc[:, gt, :],
                                        in1=fco_bb[:, :], op=ALU.add)
            ev_init = sV.inc(ins)

            aTs = attnT_hbm.rearrange("(c p) t -> p c t", p=128)
            ev_c0mm = [0] * NOT
            ev_c0ev = [0] * NOT
            ev_ald = [0] * NOT
            for gt in range(NOT):
                if gt >= 2:
                    SP.wait_ge(sP.h, ev_c0mm[gt - 2])
                ev_ald[gt] = dS.dma(SP, aT_ld[:, gt % 2, :, :],
                                    aTs[:, :, gt * 128:(gt + 1) * 128])
                PE.wait_ge(dS.h, ev_ald[gt])
                if gt == 0:
                    PE.wait_ge(dG.h, ev_fco)
                if gt >= 2:
                    PE.wait_ge(sV.h, ev_c0ev[gt - 2])
                for half in range(2):
                    ps = psB[:, (gt % 2) * 1024 + half * 512:
                             (gt % 2) * 1024 + (half + 1) * 512]
                    for c in range(NCH):
                        ins = PE.matmul(ps, aT_ld[:, gt % 2, c, :],
                                        fco_view[:, c, half * 512:(half + 1) * 512],
                                        start=(c == 0), stop=(c == NCH - 1))
                ev_c0mm[gt] = sP.inc(ins)
                DVE.wait_ge(sP.h, ev_c0mm[gt])
                if gt == 0:
                    DVE.wait_ge(sV.h, ev_init)
                DVE.drain()
                for half in range(2):
                    ps = psB[:, (gt % 2) * 1024 + half * 512:
                             (gt % 2) * 1024 + (half + 1) * 512]
                    ins = DVE.tensor_tensor(
                        out=out_acc[:, gt, half * 512:(half + 1) * 512],
                        in0=ps, in1=out_acc[:, gt, half * 512:(half + 1) * 512],
                        op=ALU.add)
                ev_c0ev[gt] = sV.inc(ins)

            if phases == 3:
                SP.wait_ge(sV.h, ev_c0ev[NOT - 1])
                dO.dma(SP, out.rearrange("(t p) f -> p t f", p=128), out_acc[:, :, :])
                SP.wait_ge(dO.h, dO.n)
                return nc

            ev_we = [0] * NE
            ev_w1 = [0] * NE
            ev_hmm_last = [0] * NE
            ev_hev = {}
            ev_ymm = {}
            ev_yev = {}
            last_yev = 0
            for e in range(NE):
                if e == 0:
                    GP.wait_ge(sP.h, ev_c0mm[NOT - 1])
                else:
                    GP.wait_ge(sP.h, ev_hmm_last[e - 1])
                ev_w1[e] = dG.dma(GP, w1buf[:, :, :],
                                  w1[e].rearrange("(c p) h -> p c h", p=128))
                if e > 0:
                    GP.wait_ge(sP.h, ev_ymm[(e - 1, 3)])
                ev_we[e] = dG.dma(GP, w2buf[:, :, :],
                                  w2[e].rearrange("(c p) d -> p c d", p=128))

                for tt in range(4):
                    for Hc in range(NHC):
                        ps = psA[:, (Hc % 4) * 512:(Hc % 4) * 512 + 512]
                        if Hc == 0 and tt == 0:
                            PE.wait_ge(dG.h, ev_w1[e])
                            if e == 0:
                                PE.wait_ge(sA.h, ev_b1e)
                        if Hc >= 4:
                            PE.wait_ge(sA.h, ev_hev[(e, tt, Hc - 4)])
                        elif tt > 0:
                            PE.wait_ge(sA.h, ev_hev[(e, tt - 1, NHC - 4 + Hc)])
                        elif e > 0:
                            PE.wait_ge(sA.h, ev_hev[(e - 1, 3, NHC - 4 + Hc)])
                        for d in range(NCH):
                            ins = PE.matmul(ps, w1buf[:, d, Hc * 128:(Hc + 1) * 128],
                                            xnT_own[:, d, tt * 512:(tt + 1) * 512],
                                            start=(d == 0), stop=(d == NCH - 1))
                        evh = sP.inc(ins)
                        ev_hmm_last[e] = evh
                        ACT.wait_ge(sP.h, evh)
                        if tt > 0:
                            ACT.wait_ge(sP.h, ev_ymm[(e, tt - 1)])
                        elif e > 0:
                            ACT.wait_ge(sP.h, ev_ymm[(e - 1, 3)])
                        ev_hev[(e, tt, Hc)] = sA.inc(ACT.activation(
                            hT[:, Hc, :], ps, AF.Relu,
                            bias=b1T[:, e * NHC + Hc: e * NHC + Hc + 1]))
                    for ttok in range(4):
                        gt = tt * 4 + ttok
                        for fc in range(2):
                            bslot = (ttok * 2 + fc) % 4
                            ps = psB[:, bslot * 512:(bslot + 1) * 512]
                            PE.wait_ge(sA.h, ev_hev[(e, tt, NHC - 1)])
                            if ttok >= 2:
                                pk = ev_yev[(e, tt, ttok - 2, fc)]
                            elif tt > 0:
                                pk = ev_yev[(e, tt - 1, ttok + 2, fc)]
                            elif e > 0:
                                pk = ev_yev[(e - 1, 3, ttok + 2, fc)]
                            else:
                                pk = ev_c0ev[NOT - 1]
                            PE.wait_ge(sV.h, pk)
                            if tt == 0 and ttok == 0 and fc == 0:
                                PE.wait_ge(dG.h, ev_we[e])
                                if e == 0:
                                    PE.wait_ge(dG.h, ev_b2r)
                            PE.matmul(ps, ones1b[:, :],
                                      b2rows[:, e * D + fc * 512: e * D + (fc + 1) * 512],
                                      start=True, stop=False)
                            for Hc in range(NHC):
                                ins = PE.matmul(ps, hT[:, Hc, ttok * 128:(ttok + 1) * 128],
                                                w2buf[:, Hc, fc * 512:(fc + 1) * 512],
                                                start=False, stop=(Hc == NHC - 1))
                            ev_ymm[(e, tt)] = sP.inc(ins)
                            DVE.wait_ge(sP.h, ev_ymm[(e, tt)])
                            DVE.drain()
                            ev_yev[(e, tt, ttok, fc)] = sV.inc(DVE.scalar_tensor_tensor(
                                out=out_acc[:, gt, fc * 512:(fc + 1) * 512],
                                in0=ps, scalar=W_sb[:, gt, e:e + 1],
                                in1=out_acc[:, gt, fc * 512:(fc + 1) * 512],
                                op0=ALU.mult, op1=ALU.add))
                            last_yev = ev_yev[(e, tt, ttok, fc)]

            SP.wait_ge(sV.h, last_yev)
            dO.dma(SP, out.rearrange("(t p) f -> p t f", p=128), out_acc[:, :, :])
            SP.wait_ge(dO.h, dO.n)

    return nc


def kernel(**inputs):
    x_full = np.ascontiguousarray(inputs["L0_attn_z"])
    keymap = {
        "wq": "L0_attn_wq", "wk": "L0_attn_wk", "wv": "L0_attn_wv",
        "fcq_w": "L0_attn_fcq_w", "fcq_b": "L0_attn_fcq_b",
        "fck_w": "L0_attn_fck_w", "fck_b": "L0_attn_fck_b",
        "fcv_w": "L0_attn_fcv_w", "fcv_b": "L0_attn_fcv_b",
        "fco_w": "L0_attn_fco_w", "fco_b": "L0_attn_fco_b",
        "w_router": "L0_ffn_w_router", "b_router": "L0_ffn_b_router",
        "w1": "L0_ffn_w1", "b1": "L0_ffn_b1", "w2": "L0_ffn_w2", "b2": "L0_ffn_b2",
    }
    weights = {k: np.ascontiguousarray(inputs[v]) for k, v in keymap.items()}

    nc = build_nc()
    in_maps = []
    for core in range(NC):
        b, q = core // 4, core % 4
        xb = x_full[b]
        own = xb[q * OWN:(q + 1) * OWN]
        rest = np.concatenate([xb[:q * OWN], xb[(q + 1) * OWN:]], axis=0)
        xc = np.ascontiguousarray(np.concatenate([own, rest], axis=0))
        m = {"x": xc}
        m.update(weights)
        in_maps.append(m)

    res = run_bass_kernel_spmd(nc, in_maps, core_ids=list(range(NC)))
    outp = np.empty((B, S, D), dtype=np.float32)
    for core in range(NC):
        b, q = core // 4, core % 4
        outp[b, q * OWN:(q + 1) * OWN] = res.results[core]["out"]
    return outp


# version 16
# speedup vs baseline: 1.6288x; 1.0693x over previous
import contextlib
import numpy as np

import concourse.bass as bass
import concourse.mybir as mybir
from concourse.bass_utils import run_bass_kernel_spmd

F32 = mybir.dt.float32
BF16 = mybir.dt.bfloat16
AF = mybir.ActivationFunctionType
ALU = mybir.AluOpType
AX = mybir.AxisListType

B, S, D = 2, 8192, 1024
NH, DK = 16, 64
NE, HF = 8, 2048
NC = 8
OWN = 2048
NT = S // 128
NOT = OWN // 128
NCH = D // 128
NHC = HF // 128
NG = NT // 4


class Sem:
    def __init__(self, ctx, nc, name):
        self.h = ctx.enter_context(nc.semaphore(name))
        self.n = 0

    def inc(self, instr, k=1):
        instr.then_inc(self.h, k)
        self.n += k
        return self.n

    def dma(self, eng, dst, src):
        eng.wait_ge(self.h, self.n)
        ins = eng.dma_start(dst, src)
        return self.inc(ins, 16)


def build_nc(phases=4):
    nc = bass.Bass("TRN2", target_bir_lowering=False)

    x = nc.dram_tensor("x", [S, D], F32, kind="ExternalInput")
    wq = nc.dram_tensor("wq", [D, DK], F32, kind="ExternalInput")
    wk = nc.dram_tensor("wk", [D, DK], F32, kind="ExternalInput")
    wv = nc.dram_tensor("wv", [D, DK], F32, kind="ExternalInput")
    fcq_w = nc.dram_tensor("fcq_w", [DK, D], F32, kind="ExternalInput")
    fck_w = nc.dram_tensor("fck_w", [DK, D], F32, kind="ExternalInput")
    fcv_w = nc.dram_tensor("fcv_w", [DK, D], F32, kind="ExternalInput")
    fcq_b = nc.dram_tensor("fcq_b", [D], F32, kind="ExternalInput")
    fck_b = nc.dram_tensor("fck_b", [D], F32, kind="ExternalInput")
    fcv_b = nc.dram_tensor("fcv_b", [D], F32, kind="ExternalInput")
    fco_w = nc.dram_tensor("fco_w", [D, D], F32, kind="ExternalInput")
    fco_b = nc.dram_tensor("fco_b", [D], F32, kind="ExternalInput")
    w_router = nc.dram_tensor("w_router", [D, NE], F32, kind="ExternalInput")
    b_router = nc.dram_tensor("b_router", [NE], F32, kind="ExternalInput")
    w1 = nc.dram_tensor("w1", [NE, D, HF], F32, kind="ExternalInput")
    b1 = nc.dram_tensor("b1", [NE, HF], F32, kind="ExternalInput")
    w2 = nc.dram_tensor("w2", [NE, HF, D], F32, kind="ExternalInput")
    b2 = nc.dram_tensor("b2", [NE, D], F32, kind="ExternalInput")
    out = nc.dram_tensor("out", [OWN, D], F32, kind="ExternalOutput")
    attnT_hbm = nc.dram_tensor("attnT_hbm", [D, OWN], BF16, kind="Internal")

    xp = x.rearrange("(n p) f -> p n f", p=128)

    PE, ACT, DVE, SP, GP = nc.tensor, nc.scalar, nc.vector, nc.sync, nc.gpsimd

    with contextlib.ExitStack() as ctx:
        E = ctx.enter_context

        ident = E(nc.sbuf_tensor("ident", [128, 128], F32))
        ones1b = E(nc.sbuf_tensor("ones1b", [1, 128], BF16))
        ones1f = E(nc.sbuf_tensor("ones1f", [1, 128], F32))
        ones64b = E(nc.sbuf_tensor("ones64b", [64, 1], BF16))
        eps_t = E(nc.sbuf_tensor("eps_t", [128, 1], F32))
        xnT_own = E(nc.sbuf_tensor("xnT_own", [128, NCH, OWN], BF16))
        W_sb = E(nc.sbuf_tensor("W_sb", [128, NOT, NE], F32))
        b1T = E(nc.sbuf_tensor("b1T", [128, 128], F32))
        b1_raw = E(nc.sbuf_tensor("b1_raw", [128, 128], F32))
        brout_row = E(nc.sbuf_tensor("brout_row", [1, NE], F32))

        psA = E(nc.psum_tensor("psA", [128, 2048], F32))
        psB = E(nc.psum_tensor("psB", [128, 2048], F32))

        dS = Sem(ctx, nc, "dS")
        dG = Sem(ctx, nc, "dG")
        dO = Sem(ctx, nc, "dO")
        sP = Sem(ctx, nc, "sP")
        sA = Sem(ctx, nc, "sA")
        sV = Sem(ctx, nc, "sV")
        sG = Sem(ctx, nc, "sG")

        blk = E(nc.Block())

        GP.memset(ident[:, :], 0.0)
        GP.drain()
        GP.affine_select(out=ident[:, :], in_=ident[:, :],
                         compare_op=ALU.not_equal, fill=1.0, base=0,
                         pattern=[[-1, 128]], channel_multiplier=1)
        GP.memset(ones1b[:, :], 1.0)
        GP.memset(ones1f[:, :], 1.0)
        GP.memset(ones64b[:, :], 1.0)
        GP.memset(eps_t[:, :], 1e-5)

        dS.dma(SP, brout_row[:, :], b_router[None, :])
        dS.dma(SP, b1_raw[:, :], b1.rearrange("e (q a) -> (e q) a", a=128))
        ev_small = dS.n

        with contextlib.ExitStack() as ab:
            EA = ab.enter_context
            t1kT = EA(nc.sbuf_tensor("t1kT", [128, S], BF16))
            t1vT = EA(nc.sbuf_tensor("t1vT", [128, S], BF16))
            t1qT = EA(nc.sbuf_tensor("t1qT", [128, OWN], BF16))
            fcq_aug = EA(nc.sbuf_tensor("fcq_aug", [128, D], BF16))
            fck_aug = EA(nc.sbuf_tensor("fck_aug", [128, D], BF16))
            fcv_aug = EA(nc.sbuf_tensor("fcv_aug", [128, D], BF16))
            GP.memset(t1kT[64:128, :], 0.0)
            GP.memset(t1vT[64:128, :], 0.0)
            GP.memset(t1qT[64:128, :], 0.0)
            GP.memset(fcq_aug[64:128, :], 0.0)
            GP.memset(fck_aug[64:128, :], 0.0)
            GP.memset(fcv_aug[64:128, :], 0.0)
            kT_2 = EA(nc.sbuf_tensor("kT_2", [128, 2, S], BF16))
            qT_2 = EA(nc.sbuf_tensor("qT_2", [128, 2, OWN], BF16))
            V_pair = EA(nc.sbuf_tensor("V_pair", [128, NT, 130], BF16))
            fcv_pack = EA(nc.sbuf_tensor("fcv_pack", [128, 8, 130], BF16))
            GP.memset(kT_2[64:128, :, :], 0.0)
            GP.memset(qT_2[64:128, :, :], 0.0)
            GP.drain()
            GP.memset(t1kT[64:65, :], 1.0)
            GP.memset(t1vT[64:65, :], 1.0)
            GP.memset(kT_2[64:65, :, :], -1.0)
            ev_consts = sG.inc(GP.memset(t1qT[64:65, :], 1.0))

            with contextlib.ExitStack() as aa:
                EB = aa.enter_context
                wqkv_bf = EB(nc.sbuf_tensor("wqkv_bf", [128, NCH, 4 * DK], BF16))
                w_rout_f = EB(nc.sbuf_tensor("w_rout_f", [128, NCH, NE], F32))
                xg = EB(nc.sbuf_tensor("xg", [128, 2, D], F32))
                xn_f = EB(nc.sbuf_tensor("xn_f", [128, 2, D], F32))
                xnT_st = EB(nc.sbuf_tensor("xnT_st", [128, 2, NCH, 512], BF16))
                xnTf_st = EB(nc.sbuf_tensor("xnTf_st", [128, 2, NCH, 128], F32))
                mvst = EB(nc.sbuf_tensor("mvst", [128, 12], F32))
                lnt = EB(nc.sbuf_tensor("lnt", [128, 2, 6], F32))
                xc_f = EB(nc.sbuf_tensor("xc_f", [128, 2, D], F32))
                rl_sc = EB(nc.sbuf_tensor("rl_sc", [128, 8, NE], F32))
                rst = EB(nc.sbuf_tensor("rst", [128, 4], F32))

                GP.memset(wqkv_bf[:, :, DK:2 * DK], 0.0)
                GP.memset(fcv_pack[:, :, 64:65], 0.0)
                GP.memset(fcv_pack[:, :, 129:130], 0.0)
                GP.drain()
                GP.memset(fcv_pack[64:65, :, 64:65], 1.0)
                ev_fvp0 = sG.inc(GP.memset(fcv_pack[64:65, :, 129:130], 1.0))
                dG.dma(GP, wqkv_bf[:, :, 0:DK], wq.rearrange("(c p) k -> p c k", p=128))
                dG.dma(GP, wqkv_bf[:, :, 2 * DK:3 * DK], wk.rearrange("(c p) k -> p c k", p=128))
                dG.dma(GP, wqkv_bf[:, :, 3 * DK:4 * DK], wv.rearrange("(c p) k -> p c k", p=128))
                ev_wqkv = dG.n
                for wsrc, bsrc, dst in ((fcq_w, fcq_b, fcq_aug), (fck_w, fck_b, fck_aug),
                                        (fcv_w, fcv_b, fcv_aug)):
                    dG.dma(GP, dst[0:64, :], wsrc[:, :])
                    dG.dma(GP, dst[64:65, :], bsrc[None, :])
                ev_wA = dG.n
                ev_wrout = dS.dma(SP, w_rout_f[:, :, :],
                                  w_router.rearrange("(c p) e -> p c e", p=128))

                ev_xdma = [0] * NT
                ev_xn = [0] * NT
                ev_sub = [0] * NT
                ev_tp = [0] * NT
                ev_evac = [0] * NT
                ev_evacf = [0] * NT
                ev_t1 = [0] * NG
                ev_t1ev = [0] * NG
                ev_rlog = [0] * NOT
                ev_rW = [0] * NOT

                def stage1(t):
                    own = t < NOT
                    if t == 0:
                        ACT.wait_ge(sG.h, ev_consts)
                    if t >= 2:
                        SP.wait_ge(sV.h, ev_sub[t - 2])
                    ev_xdma[t] = dS.dma(SP, xg[:, t % 2, :], xp[:, t, :])
                    xtile = xg[:, t % 2, :]
                    lb = lnt[:, t % 2, :]
                    DVE.wait_ge(dS.h, ev_xdma[t])
                    if t >= 2:
                        DVE.wait_ge(sA.h, ev_xn[t - 2])
                    if t > 0:
                        DVE.drain()
                    DVE.bn_stats(mvst[:, 0:6], xtile[:, 0:512])
                    DVE.bn_stats(mvst[:, 6:12], xtile[:, 512:1024])
                    DVE.drain()
                    ev1 = sV.inc(DVE.bn_aggr(lb[:, 0:2], mvst[:, 0:12]))
                    DVE.drain()
                    ev_sub[t] = sV.inc(DVE.tensor_scalar(
                        out=xc_f[:, t % 2, :], in0=xtile,
                        scalar1=lb[:, 0:1], scalar2=None, op0=ALU.subtract))
                    ACT.wait_ge(sV.h, ev1)
                    if t > 0:
                        ACT.drain()
                    ACT.activation(lb[:, 4:5], lb[:, 1:2], AF.Ln, bias=eps_t[:, :])
                    ACT.drain()
                    ACT.activation(lb[:, 2:3], lb[:, 4:5], AF.Exp, scale=-0.5)
                    ACT.drain()
                    ACT.wait_ge(sV.h, ev_sub[t])
                    ev_xn[t] = sA.inc(ACT.activation(xn_f[:, t % 2, :],
                                                     xc_f[:, t % 2, :],
                                                     AF.Identity, scale=lb[:, 2:3]))

                def stage2(t):
                    own = t < NOT
                    g4 = t // 4
                    if t == 0:
                        PE.wait_ge(sG.h, ev_consts)
                    PE.wait_ge(sA.h, ev_xn[t])
                    if t >= 2:
                        if t - 2 < NOT:
                            PE.wait_ge(sA.h, ev_evacf[t - 2])
                        else:
                            PE.wait_ge(sV.h, ev_evac[t - 2])
                    base = (t % 2) * 1024
                    for c in range(NCH):
                        off = base + (c // 4) * 512 + (c % 4) * 128
                        ins = PE.transpose(psA[:, off:off + 128],
                                           xn_f[:, t % 2, c * 128:(c + 1) * 128],
                                           ident[:, :])
                    ev_tp[t] = sP.inc(ins)
                    if own:
                        ACT.wait_ge(sP.h, ev_tp[t])
                        for j in range(2):
                            ins = ACT.activation(
                                xnTf_st[:, t % 2, j * 4:(j + 1) * 4, :],
                                psA[:, base + j * 512: base + j * 512 + 512], AF.Copy)
                        ev_evacf[t] = sA.inc(ins)
                        DVE.wait_ge(sA.h, ev_evacf[t])
                        ins = DVE.tensor_copy(
                            xnT_own[:, :, t * 128:(t + 1) * 128],
                            xnTf_st[:, t % 2, :, :])
                        ev_evac[t] = sV.inc(ins)
                    else:
                        DVE.wait_ge(sP.h, ev_tp[t])
                        if g4 >= 2:
                            DVE.wait_ge(sP.h, ev_t1[g4 - 2])
                        for j in range(2):
                            srcp = psA[:, base + j * 512: base + j * 512 + 512]
                            dst = xnT_st[:, g4 % 2, j * 4:(j + 1) * 4,
                                         (t % 4) * 128:(t % 4) * 128 + 128]
                            ins = DVE.tensor_copy(dst, srcp)
                        ev_evac[t] = sV.inc(ins)
                        ev_evacf[t] = 0
                    if t % 4 == 3:
                        PE.wait_ge(sV.h, ev_evac[t])
                        if g4 == 0:
                            PE.wait_ge(dG.h, ev_wqkv)
                        if g4 >= 1:
                            PE.wait_ge(sA.h, ev_t1ev[g4 - 1])
                        projs = [(2, psB[:, 0:512])]
                        if own:
                            projs.append((0, psB[:, 512:1024]))
                        for wi, ps in projs:
                            for c in range(NCH):
                                if own:
                                    rhs = xnT_own[:, c, g4 * 512:(g4 + 1) * 512]
                                else:
                                    rhs = xnT_st[:, g4 % 2, c, :]
                                ins = PE.matmul(ps, wqkv_bf[:, c, wi * DK:(wi + 2) * DK],
                                                rhs, start=(c == 0), stop=(c == NCH - 1))
                        ev_t1[g4] = sP.inc(ins)
                        ACT.wait_ge(sP.h, ev_t1[g4])
                        ins = ACT.activation(t1kT[0:64, g4 * 512:(g4 + 1) * 512],
                                             psB[0:64, 0:512], AF.Copy)
                        ins = ACT.activation(t1vT[0:64, g4 * 512:(g4 + 1) * 512],
                                             psB[64:128, 0:512], AF.Copy)
                        if own:
                            ins = ACT.activation(t1qT[0:64, g4 * 512:(g4 + 1) * 512],
                                                 psB[0:64, 512:1024], AF.Copy)
                        ev_t1ev[g4] = sA.inc(ins)

                def stage3(t):
                    slot = t % 4
                    rps = psB[:, 1536 + slot * NE: 1536 + (slot + 1) * NE]
                    PE.wait_ge(sA.h, ev_evacf[t])
                    if t == 0:
                        PE.wait_ge(dS.h, ev_wrout)
                    else:
                        PE.wait_ge(sV.h, ev_rlog[t - 1])
                    PE.matmul(rps, ones1f[:, :], brout_row[:, :], start=True, stop=False)
                    for c in range(NCH):
                        ins = PE.matmul(rps, xnTf_st[:, t % 2, c, :], w_rout_f[:, c, :],
                                        start=False, stop=(c == NCH - 1))
                    evr = sP.inc(ins)
                    DVE.wait_ge(sP.h, evr)
                    if t >= 4:
                        DVE.wait_ge(sV.h, ev_rW[t - 4])
                    DVE.drain()
                    DVE.tensor_copy(rl_sc[:, slot, :], rps)
                    DVE.drain()
                    DVE.reduce_max(rst[:, 0:1], rl_sc[:, slot, :], axis=AX.X)
                    DVE.drain()
                    ev_rlog[t] = sV.inc(DVE.tensor_scalar(
                        out=rst[:, 1:2], in0=rst[:, 0:1],
                        scalar1=-1.0, scalar2=None, op0=ALU.mult))
                    ACT.wait_ge(sV.h, ev_rlog[t])
                    ACT.drain()
                    ev5 = sA.inc(ACT.activation(rl_sc[:, 4 + slot, :], rl_sc[:, slot, :],
                                                AF.Exp, bias=rst[:, 1:2]))
                    DVE.wait_ge(sA.h, ev5)
                    DVE.drain()
                    DVE.reduce_sum(rst[:, 2:3], rl_sc[:, 4 + slot, :], axis=AX.X)
                    DVE.drain()
                    DVE.reciprocal(rst[:, 3:4], rst[:, 2:3])
                    DVE.drain()
                    ev_rW[t] = sV.inc(DVE.tensor_scalar(
                        out=W_sb[:, t, :], in0=rl_sc[:, slot, :],
                        scalar1=rst[:, 0:1], scalar2=rst[:, 3:4],
                        op0=ALU.is_equal, op1=ALU.mult))

                for t in range(NT + 2):
                    if t < NT:
                        stage1(t)
                    if 1 <= t <= NT:
                        stage2(t - 1)
                    if 2 <= t and t - 2 < NOT:
                        stage3(t - 2)

                last_t1ev = ev_t1ev[NG - 1]
                last_A_evac = ev_evac[NT - 1]

                if phases in (1, 9, 10, 11, 12, 13):
                    if phases in (9, 10):
                        SP.wait_ge(sA.h, sA.n)
                        SP.wait_ge(sV.h, sV.n)
                        return nc
                    SP.wait_ge(sA.h, sA.n)
                    SP.wait_ge(sV.h, sV.n)
                    return nc

            bb = contextlib.ExitStack()
            EBB = bb.enter_context
            pTf = EBB(nc.sbuf_tensor("pTf", [128, 2048], BF16))
            sq2k = EBB(nc.sbuf_tensor("sq2k", [64, 2, S], BF16))
            sq2q = EBB(nc.sbuf_tensor("sq2q", [64, 2, OWN], BF16))
            colmax = EBB(nc.sbuf_tensor("colmax", [64, 2], BF16))
            mk2p = EBB(nc.sbuf_tensor("mk2p", [1, 2], F32))
            r_row = EBB(nc.sbuf_tensor("r_row", [1, 512], F32))
            atnU = EBB(nc.sbuf_tensor("atnU", [64, 512], BF16))
            r_den = EBB(nc.sbuf_tensor("r_den", [1, 512], F32))
            rbc_f32 = EBB(nc.sbuf_tensor("rbc_f32", [64, 512], F32))
            atn_st = EBB(nc.sbuf_tensor("atn_st", [64, 2, OWN], BF16))

            NPASS = NH * 4
            NSTEP = NPASS * NT
            ev_qk = [0] * NSTEP
            ev_expp = [0] * (NSTEP // 2)
            ev_av = [0] * NSTEP
            ev_epi = [0] * NPASS
            ev_den = [0] * NPASS
            ev_spill = [0] * NH
            ev_kev = {}
            ev_qev = {}
            ev_row_h = [0] * NH
            ev_nq_h = [0] * NH
            ev_q2g = [0] * (NH // 2)
            ev_cm_h = [0] * (NH // 2)

            DVE.wait_ge(dG.h, ev_wA)
            for p8 in range(8):
                DVE.drain()
                DVE.tensor_copy(fcv_pack[:, p8, 0:64],
                                fcv_aug[:, p8 * 128:p8 * 128 + 64])
                DVE.drain()
                ins = DVE.tensor_copy(fcv_pack[:, p8, 65:129],
                                      fcv_aug[:, p8 * 128 + 64:p8 * 128 + 128])
            ev_fvp = sV.inc(ins)

            def prologue(p):
                h = 2 * p
                s0 = p * 8 * NT
                if p == 0:
                    PE.wait_ge(dG.h, ev_wA)
                    PE.wait_ge(sA.h, last_t1ev)
                    PE.wait_ge(sV.h, last_A_evac)
                else:
                    PE.wait_ge(sA.h, ev_expp[(s0 - 2) // 2])
                    ACT.wait_ge(sG.h, ev_q2g[p - 1])
                    DVE.wait_ge(sG.h, ev_q2g[p - 1])
                hc2 = slice(h * DK, (h + 2) * DK)
                for j in range(16):
                    if j >= 4:
                        if (j - 4) % 4 < 2:
                            PE.wait_ge(sA.h, ev_kev[(h, j - 4)])
                        else:
                            PE.wait_ge(sV.h, ev_kev[(h, j - 4)])
                    ins = PE.matmul(psA[:, (j % 4) * 512:(j % 4) * 512 + 512],
                                    fck_aug[:, hc2],
                                    t1kT[:, j * 512:(j + 1) * 512],
                                    start=True, stop=True)
                    evk = sP.inc(ins)
                    eng = ACT if (j % 4) < 2 else DVE
                    eng.wait_ge(sP.h, evk)
                    if eng is DVE:
                        eng.drain()
                    for pj in range(2):
                        srck = psA[pj * 64:(pj + 1) * 64,
                                   (j % 4) * 512:(j % 4) * 512 + 512]
                        dstk = kT_2[0:64, pj, j * 512:(j + 1) * 512]
                        if eng is ACT:
                            ins2 = eng.activation(dstk, srck, AF.Copy)
                        else:
                            ins2 = eng.tensor_copy(dstk, srck)
                    ev_kev[(h, j)] = (sA if eng is ACT else sV).inc(ins2)
                for jq in range(4):
                    prev = ev_kev[(h, 12 + jq)]
                    if (12 + jq) % 4 < 2:
                        PE.wait_ge(sA.h, prev)
                    else:
                        PE.wait_ge(sV.h, prev)
                    ins = PE.matmul(psA[:, jq * 512:jq * 512 + 512],
                                    fcq_aug[:, hc2],
                                    t1qT[:, jq * 512:(jq + 1) * 512],
                                    start=True, stop=True)
                    evq = sP.inc(ins)
                    eng = ACT if (jq % 4) < 2 else DVE
                    eng.wait_ge(sP.h, evq)
                    if eng is DVE:
                        eng.drain()
                    for parq in range(2):
                        srcq = psA[parq * 64:(parq + 1) * 64,
                                   jq * 512:jq * 512 + 512]
                        dstq = qT_2[0:64, parq, jq * 512:(jq + 1) * 512]
                        if eng is ACT:
                            ins = eng.activation(dstq, srcq, AF.Copy)
                        else:
                            ins = eng.tensor_copy(dstq, srcq)
                    ev_qev[(h, jq)] = (sA if eng is ACT else sV).inc(ins)

                ev_vev = [0] * NT
                for kt in range(NT):
                    b = kt % 4
                    if kt == 0:
                        PE.wait_ge(sA.h, ev_qev[(h, 1)])
                        PE.wait_ge(sV.h, ev_qev[(h, 3)])
                        if p == 0:
                            PE.wait_ge(sV.h, ev_fvp)
                            PE.wait_ge(sG.h, ev_fvp0)
                    elif kt >= 4:
                        PE.wait_ge(sV.h, ev_vev[kt - 4])
                    ins = PE.matmul(psA[:, b * 512:b * 512 + 130],
                                    t1vT[:, kt * 128:(kt + 1) * 128],
                                    fcv_pack[:, p, :], start=True, stop=True)
                    evv = sP.inc(ins)
                    DVE.wait_ge(sP.h, evv)
                    DVE.drain()
                    ev_vev[kt] = sV.inc(DVE.tensor_copy(
                        V_pair[:, kt, :], psA[:, b * 512:b * 512 + 130]))

                GP.wait_ge(sA.h, ev_kev[(h, 13)])
                GP.wait_ge(sV.h, ev_kev[(h, 15)])
                GP.wait_ge(sA.h, ev_qev[(h, 1)])
                GP.wait_ge(sV.h, ev_qev[(h, 3)])
                if p > 0:
                    GP.wait_ge(sP.h, ev_nq_h[h - 1])
                    GP.wait_ge(sV.h, ev_cm_h[p - 1])
                    DVE.wait_ge(sP.h, ev_nq_h[h - 1])
                evsq = [0, 0]
                for hh in range(2):
                    GP.drain()
                    evsq[hh] = sG.inc(GP.tensor_tensor(
                        out=sq2k[:, hh, :], in0=kT_2[0:64, hh, :],
                        in1=kT_2[0:64, hh, :], op=ALU.mult))
                for hh in range(2):
                    GP.drain()
                    ins = GP.tensor_tensor(out=sq2q[:, hh, :],
                                           in0=qT_2[0:64, hh, :],
                                           in1=qT_2[0:64, hh, :], op=ALU.mult)
                ev_q2g[p] = sG.inc(ins)
                for hh in range(2):
                    DVE.wait_ge(sG.h, evsq[hh])
                    DVE.drain()
                    ins = DVE.reduce_max(colmax[:, hh:hh + 1], sq2k[:, hh, :],
                                         axis=AX.X)
                ev_cm_h[p] = sV.inc(ins)

                PE.wait_ge(sV.h, ev_cm_h[p])
                ins = PE.matmul(psA[0:1, 384:386], ones64b[:, :], colmax[:, :],
                                start=True, stop=True)
                ev_mk = sP.inc(ins)
                DVE.wait_ge(sP.h, ev_mk)
                DVE.drain()
                ev_mkc = sV.inc(DVE.tensor_copy(mk2p[:, :], psA[0:1, 384:386]))
                for hh in range(2):
                    PE.wait_ge(sG.h, ev_q2g[p])
                    if hh == 0:
                        PE.wait_ge(sV.h, ev_mkc)
                    else:
                        PE.wait_ge(sA.h, ev_row_h[h])
                    for jq in range(4):
                        ins = PE.matmul(psA[0:1, jq * 512:jq * 512 + 512],
                                        ones64b[:, :],
                                        sq2q[:, hh, jq * 512:(jq + 1) * 512],
                                        start=True, stop=True)
                    ev_nq_h[h + hh] = sP.inc(ins)
                    ACT.wait_ge(sP.h, ev_nq_h[h + hh])
                    if hh == 0:
                        ACT.wait_ge(sV.h, ev_mkc)
                    ev_row_h[h + hh] = sA.inc(ACT.activation(
                        qT_2[64:65, hh, :], psA[0:1, 0:2048],
                        AF.Sqrt, scale=mk2p[:, hh:hh + 1]))

            def emit_qk(s):
                pss, kt = divmod(s, NT)
                head, qp = divmod(pss, 4)
                par = head % 2
                b = s % 4
                if s >= 4:
                    PE.wait_ge(sA.h, ev_expp[(s - 4) // 2])
                if s % (8 * NT) < 4:
                    PE.wait_ge(sA.h, ev_row_h[(s // (8 * NT)) * 2 + 1])
                ins = PE.matmul(psA[:, b * 512:(b + 1) * 512],
                                kT_2[:, par, kt * 128:(kt + 1) * 128],
                                qT_2[:, par, qp * 512:(qp + 1) * 512],
                                start=True, stop=True)
                ev_qk[s] = sP.inc(ins)

            def emit_expp(j):
                b = (2 * j) % 4
                ACT.wait_ge(sP.h, ev_qk[2 * j + 1])
                ev_expp[j] = sA.inc(ACT.activation(
                    pTf[:, b * 512:b * 512 + 1024],
                    psA[:, b * 512:b * 512 + 1024], AF.Exp, scale=0.125))

            def emit_av(s):
                pss, kt = divmod(s, NT)
                head = pss // 4
                hh = head % 2
                b = pss % 4
                PE.wait_ge(sA.h, ev_expp[s // 2])
                if kt == 0 and pss >= 4:
                    PE.wait_ge(sV.h, ev_epi[pss - 4])
                ins = PE.matmul(psB[0:65, b * 512:(b + 1) * 512],
                                V_pair[:, kt, hh * 65:hh * 65 + 65],
                                pTf[:, (s % 4) * 512:(s % 4) * 512 + 512],
                                start=(kt == 0), stop=(kt == NT - 1))
                ev_av[s] = sP.inc(ins)

            def emit_epiA(pss):
                b = pss % 4
                DVE.wait_ge(sP.h, ev_av[pss * NT + NT - 1])
                DVE.drain()
                DVE.tensor_copy(atnU[:, :], psB[0:64, b * 512:(b + 1) * 512])
                DVE.drain()
                ev_den[pss] = sV.inc(DVE.tensor_copy(
                    r_den[:, :], psB[64:65, b * 512:(b + 1) * 512]))

            def emit_epiB(pss):
                head, qp = divmod(pss, 4)
                par = head % 2
                b = pss % 4
                PE.wait_ge(sV.h, ev_den[pss])
                ins = PE.matmul(psB[0:64, b * 512:(b + 1) * 512],
                                ones1f[:, 0:64], r_den[:, :],
                                start=True, stop=True)
                ev_rbc = sP.inc(ins)
                DVE.wait_ge(sP.h, ev_rbc)
                if head >= 2 and qp == 0:
                    DVE.wait_ge(dO.h, ev_spill[head - 2])
                DVE.drain()
                DVE.reciprocal(rbc_f32[:, :], psB[0:64, b * 512:(b + 1) * 512])
                DVE.drain()
                ev_epi[pss] = sV.inc(DVE.tensor_tensor(
                    out=atn_st[:, par, qp * 512:(qp + 1) * 512],
                    in0=atnU[:, :], in1=rbc_f32[:, :],
                    op=ALU.mult))
                if qp == 3:
                    SP.wait_ge(sV.h, ev_epi[pss])
                    ev_spill[head] = dO.dma(
                        SP, attnT_hbm[head * DK:(head + 1) * DK, :],
                        atn_st[:, par, :])

            for s in range(NSTEP + 8):
                if s < NSTEP and s % (8 * NT) == 0:
                    prologue(s // (8 * NT))
                if s < NSTEP:
                    emit_qk(s)
                if s >= 3 and s - 3 < NSTEP:
                    emit_av(s - 3)
                if s < NSTEP and s % 2 == 1:
                    emit_expp((s - 1) // 2)
                if s >= 3 and (s - 3) % NT == NT - 1:
                    emit_epiA((s - 3) // NT)
                if s >= 8 and (s - 8) % NT == NT - 1:
                    emit_epiB((s - 8) // NT)

            bb.close()
        snapB_P, snapB_A, snapB_V = sP.n, sA.n, sV.n

        with contextlib.ExitStack() as cc:
            EC = cc.enter_context
            out_acc = EC(nc.sbuf_tensor("out_acc", [128, NOT, D], F32))
            w1buf = EC(nc.sbuf_tensor("w1buf", [128, NCH, HF], BF16))
            w2buf = EC(nc.sbuf_tensor("w2buf", [128, NHC, D], BF16))
            hT = EC(nc.sbuf_tensor("hT", [128, NHC, 512], BF16))
            aT_ld = EC(nc.sbuf_tensor("aT_ld", [128, 2, NCH, 128], BF16))
            b2rows = EC(nc.sbuf_tensor("b2rows", [1, NE * D], BF16))
            fco_bb = EC(nc.sbuf_tensor("fco_bb", [128, D], F32))
            fco_brow = EC(nc.sbuf_tensor("fco_brow", [1, D], F32))
            fco_view = w2buf[:, 0:NCH, :]

            SP.wait_ge(dO.h, ev_spill[NH - 1])
            SP.wait_ge(sP.h, snapB_P)
            SP.wait_ge(sV.h, snapB_V)
            dS.dma(SP, out_acc[:, :, :], xp[:, 0:NOT, :])
            ev_xinit = dS.dma(SP, fco_brow[:, :], fco_b[None, :])
            GP.wait_ge(sP.h, snapB_P)
            GP.wait_ge(sV.h, snapB_V)
            GP.wait_ge(dO.h, ev_spill[NH - 1])
            dG.dma(GP, fco_view[:, :, :], fco_w.rearrange("(c p) e -> p c e", p=128))
            dG.dma(GP, b2rows[:, :], b2.rearrange("e d -> (e d)")[None, :])
            ev_fco = dG.n
            ev_b2r = dG.n
            PE.wait_ge(dS.h, ev_xinit)
            PE.wait_ge(sA.h, snapB_A)
            PE.wait_ge(sV.h, snapB_V)
            for half in range(2):
                ins = PE.matmul(psA[:, half * 512:(half + 1) * 512],
                                ones1f[:, :], fco_brow[:, half * 512:(half + 1) * 512],
                                start=True, stop=True)
            ev_bbmm = sP.inc(ins)
            ACT.wait_ge(sP.h, ev_bbmm)
            ev_bbe = sA.inc(ACT.activation(fco_bb[:, :], psA[:, 0:1024], AF.Copy))
            ins = PE.transpose(psA[:, 1024:1152], b1_raw[:, :], ident[:, :])
            ev_b1t = sP.inc(ins)
            ACT.wait_ge(sP.h, ev_b1t)
            ev_b1e = sA.inc(ACT.activation(b1T[:, :], psA[:, 1024:1152], AF.Copy))
            DVE.wait_ge(sA.h, ev_bbe)
            DVE.wait_ge(dS.h, ev_xinit)
            DVE.drain()
            for gt in range(NOT):
                ins = DVE.tensor_tensor(out=out_acc[:, gt, :], in0=out_acc[:, gt, :],
                                        in1=fco_bb[:, :], op=ALU.add)
            ev_init = sV.inc(ins)

            aTs = attnT_hbm.rearrange("(c p) t -> p c t", p=128)
            ev_c0mm = [0] * NOT
            ev_c0ev = [0] * NOT
            ev_ald = [0] * NOT
            for gt in range(NOT):
                if gt >= 2:
                    SP.wait_ge(sP.h, ev_c0mm[gt - 2])
                ev_ald[gt] = dS.dma(SP, aT_ld[:, gt % 2, :, :],
                                    aTs[:, :, gt * 128:(gt + 1) * 128])
                PE.wait_ge(dS.h, ev_ald[gt])
                if gt == 0:
                    PE.wait_ge(dG.h, ev_fco)
                if gt >= 2:
                    PE.wait_ge(sV.h, ev_c0ev[gt - 2])
                for half in range(2):
                    ps = psB[:, (gt % 2) * 1024 + half * 512:
                             (gt % 2) * 1024 + (half + 1) * 512]
                    for c in range(NCH):
                        ins = PE.matmul(ps, aT_ld[:, gt % 2, c, :],
                                        fco_view[:, c, half * 512:(half + 1) * 512],
                                        start=(c == 0), stop=(c == NCH - 1))
                ev_c0mm[gt] = sP.inc(ins)
                DVE.wait_ge(sP.h, ev_c0mm[gt])
                if gt == 0:
                    DVE.wait_ge(sV.h, ev_init)
                DVE.drain()
                for half in range(2):
                    ps = psB[:, (gt % 2) * 1024 + half * 512:
                             (gt % 2) * 1024 + (half + 1) * 512]
                    ins = DVE.tensor_tensor(
                        out=out_acc[:, gt, half * 512:(half + 1) * 512],
                        in0=ps, in1=out_acc[:, gt, half * 512:(half + 1) * 512],
                        op=ALU.add)
                ev_c0ev[gt] = sV.inc(ins)

            if phases == 3:
                SP.wait_ge(sV.h, ev_c0ev[NOT - 1])
                dO.dma(SP, out.rearrange("(t p) f -> p t f", p=128), out_acc[:, :, :])
                SP.wait_ge(dO.h, dO.n)
                return nc

            ev_we = [0] * NE
            ev_w1 = [0] * NE
            ev_hmm_last = [0] * NE
            ev_hev = {}
            ev_ymm = {}
            ev_yev = {}
            last_yev = 0
            for e in range(NE):
                if e == 0:
                    GP.wait_ge(sP.h, ev_c0mm[NOT - 1])
                else:
                    GP.wait_ge(sP.h, ev_hmm_last[e - 1])
                ev_w1[e] = dG.dma(GP, w1buf[:, :, :],
                                  w1[e].rearrange("(c p) h -> p c h", p=128))
                if e > 0:
                    GP.wait_ge(sP.h, ev_ymm[(e - 1, 3)])
                ev_we[e] = dG.dma(GP, w2buf[:, :, :],
                                  w2[e].rearrange("(c p) d -> p c d", p=128))

                for tt in range(4):
                    for Hc in range(NHC):
                        ps = psA[:, (Hc % 4) * 512:(Hc % 4) * 512 + 512]
                        if Hc == 0 and tt == 0:
                            PE.wait_ge(dG.h, ev_w1[e])
                            if e == 0:
                                PE.wait_ge(sA.h, ev_b1e)
                        if Hc >= 4:
                            PE.wait_ge(sA.h, ev_hev[(e, tt, Hc - 4)])
                        elif tt > 0:
                            PE.wait_ge(sA.h, ev_hev[(e, tt - 1, NHC - 4 + Hc)])
                        elif e > 0:
                            PE.wait_ge(sA.h, ev_hev[(e - 1, 3, NHC - 4 + Hc)])
                        for d in range(NCH):
                            ins = PE.matmul(ps, w1buf[:, d, Hc * 128:(Hc + 1) * 128],
                                            xnT_own[:, d, tt * 512:(tt + 1) * 512],
                                            start=(d == 0), stop=(d == NCH - 1))
                        evh = sP.inc(ins)
                        ev_hmm_last[e] = evh
                        ACT.wait_ge(sP.h, evh)
                        if tt > 0:
                            ACT.wait_ge(sP.h, ev_ymm[(e, tt - 1)])
                        elif e > 0:
                            ACT.wait_ge(sP.h, ev_ymm[(e - 1, 3)])
                        ev_hev[(e, tt, Hc)] = sA.inc(ACT.activation(
                            hT[:, Hc, :], ps, AF.Relu,
                            bias=b1T[:, e * NHC + Hc: e * NHC + Hc + 1]))
                    for ttok in range(4):
                        gt = tt * 4 + ttok
                        for fc in range(2):
                            bslot = (ttok * 2 + fc) % 4
                            ps = psB[:, bslot * 512:(bslot + 1) * 512]
                            PE.wait_ge(sA.h, ev_hev[(e, tt, NHC - 1)])
                            if ttok >= 2:
                                pk = ev_yev[(e, tt, ttok - 2, fc)]
                            elif tt > 0:
                                pk = ev_yev[(e, tt - 1, ttok + 2, fc)]
                            elif e > 0:
                                pk = ev_yev[(e - 1, 3, ttok + 2, fc)]
                            else:
                                pk = ev_c0ev[NOT - 1]
                            PE.wait_ge(sV.h, pk)
                            if tt == 0 and ttok == 0 and fc == 0:
                                PE.wait_ge(dG.h, ev_we[e])
                                if e == 0:
                                    PE.wait_ge(dG.h, ev_b2r)
                            PE.matmul(ps, ones1b[:, :],
                                      b2rows[:, e * D + fc * 512: e * D + (fc + 1) * 512],
                                      start=True, stop=False)
                            for Hc in range(NHC):
                                ins = PE.matmul(ps, hT[:, Hc, ttok * 128:(ttok + 1) * 128],
                                                w2buf[:, Hc, fc * 512:(fc + 1) * 512],
                                                start=False, stop=(Hc == NHC - 1))
                            ev_ymm[(e, tt)] = sP.inc(ins)
                            DVE.wait_ge(sP.h, ev_ymm[(e, tt)])
                            DVE.drain()
                            ev_yev[(e, tt, ttok, fc)] = sV.inc(DVE.scalar_tensor_tensor(
                                out=out_acc[:, gt, fc * 512:(fc + 1) * 512],
                                in0=ps, scalar=W_sb[:, gt, e:e + 1],
                                in1=out_acc[:, gt, fc * 512:(fc + 1) * 512],
                                op0=ALU.mult, op1=ALU.add))
                            last_yev = ev_yev[(e, tt, ttok, fc)]

            SP.wait_ge(sV.h, last_yev)
            dO.dma(SP, out.rearrange("(t p) f -> p t f", p=128), out_acc[:, :, :])
            SP.wait_ge(dO.h, dO.n)

    return nc


def kernel(**inputs):
    x_full = np.ascontiguousarray(inputs["L0_attn_z"])
    keymap = {
        "wq": "L0_attn_wq", "wk": "L0_attn_wk", "wv": "L0_attn_wv",
        "fcq_w": "L0_attn_fcq_w", "fcq_b": "L0_attn_fcq_b",
        "fck_w": "L0_attn_fck_w", "fck_b": "L0_attn_fck_b",
        "fcv_w": "L0_attn_fcv_w", "fcv_b": "L0_attn_fcv_b",
        "fco_w": "L0_attn_fco_w", "fco_b": "L0_attn_fco_b",
        "w_router": "L0_ffn_w_router", "b_router": "L0_ffn_b_router",
        "w1": "L0_ffn_w1", "b1": "L0_ffn_b1", "w2": "L0_ffn_w2", "b2": "L0_ffn_b2",
    }
    weights = {k: np.ascontiguousarray(inputs[v]) for k, v in keymap.items()}

    nc = build_nc()
    in_maps = []
    for core in range(NC):
        b, q = core // 4, core % 4
        xb = x_full[b]
        own = xb[q * OWN:(q + 1) * OWN]
        rest = np.concatenate([xb[:q * OWN], xb[(q + 1) * OWN:]], axis=0)
        xc = np.ascontiguousarray(np.concatenate([own, rest], axis=0))
        m = {"x": xc}
        m.update(weights)
        in_maps.append(m)

    res = run_bass_kernel_spmd(nc, in_maps, core_ids=list(range(NC)))
    outp = np.empty((B, S, D), dtype=np.float32)
    for core in range(NC):
        b, q = core // 4, core % 4
        outp[b, q * OWN:(q + 1) * OWN] = res.results[core]["out"]
    return outp


# version 19
# speedup vs baseline: 1.6404x; 1.0071x over previous
import contextlib
import numpy as np

import concourse.bass as bass
import concourse.mybir as mybir
from concourse.bass_utils import run_bass_kernel_spmd

F32 = mybir.dt.float32
BF16 = mybir.dt.bfloat16
AF = mybir.ActivationFunctionType
ALU = mybir.AluOpType
AX = mybir.AxisListType

B, S, D = 2, 8192, 1024
NH, DK = 16, 64
NE, HF = 8, 2048
NC = 8
OWN = 2048
NT = S // 128
NOT = OWN // 128
NCH = D // 128
NHC = HF // 128
NG = NT // 4


class Sem:
    def __init__(self, ctx, nc, name):
        self.h = ctx.enter_context(nc.semaphore(name))
        self.n = 0

    def inc(self, instr, k=1):
        instr.then_inc(self.h, k)
        self.n += k
        return self.n

    def dma(self, eng, dst, src):
        eng.wait_ge(self.h, self.n)
        ins = eng.dma_start(dst, src)
        return self.inc(ins, 16)


def build_nc(phases=4):
    nc = bass.Bass("TRN2", target_bir_lowering=False)

    x = nc.dram_tensor("x", [S, D], F32, kind="ExternalInput")
    wq = nc.dram_tensor("wq", [D, DK], F32, kind="ExternalInput")
    wk = nc.dram_tensor("wk", [D, DK], F32, kind="ExternalInput")
    wv = nc.dram_tensor("wv", [D, DK], F32, kind="ExternalInput")
    fcq_w = nc.dram_tensor("fcq_w", [DK, D], F32, kind="ExternalInput")
    fck_w = nc.dram_tensor("fck_w", [DK, D], F32, kind="ExternalInput")
    fcv_w = nc.dram_tensor("fcv_w", [DK, D], F32, kind="ExternalInput")
    fcq_b = nc.dram_tensor("fcq_b", [D], F32, kind="ExternalInput")
    fck_b = nc.dram_tensor("fck_b", [D], F32, kind="ExternalInput")
    fcv_b = nc.dram_tensor("fcv_b", [D], F32, kind="ExternalInput")
    fco_w = nc.dram_tensor("fco_w", [D, D], F32, kind="ExternalInput")
    fco_b = nc.dram_tensor("fco_b", [D], F32, kind="ExternalInput")
    w_router = nc.dram_tensor("w_router", [D, NE], F32, kind="ExternalInput")
    b_router = nc.dram_tensor("b_router", [NE], F32, kind="ExternalInput")
    w1 = nc.dram_tensor("w1", [NE, D, HF], F32, kind="ExternalInput")
    b1 = nc.dram_tensor("b1", [NE, HF], F32, kind="ExternalInput")
    w2 = nc.dram_tensor("w2", [NE, HF, D], F32, kind="ExternalInput")
    b2 = nc.dram_tensor("b2", [NE, D], F32, kind="ExternalInput")
    out = nc.dram_tensor("out", [OWN, D], F32, kind="ExternalOutput")
    attnT_hbm = nc.dram_tensor("attnT_hbm", [D, OWN], BF16, kind="Internal")

    xp = x.rearrange("(n p) f -> p n f", p=128)

    PE, ACT, DVE, SP, GP = nc.tensor, nc.scalar, nc.vector, nc.sync, nc.gpsimd

    with contextlib.ExitStack() as ctx:
        E = ctx.enter_context

        ident = E(nc.sbuf_tensor("ident", [128, 128], F32))
        ones1b = E(nc.sbuf_tensor("ones1b", [1, 128], BF16))
        ones1f = E(nc.sbuf_tensor("ones1f", [1, 128], F32))
        ones64b = E(nc.sbuf_tensor("ones64b", [64, 1], BF16))
        eps_t = E(nc.sbuf_tensor("eps_t", [128, 1], F32))
        xnT_own = E(nc.sbuf_tensor("xnT_own", [128, NCH, OWN], BF16))
        W_sb = E(nc.sbuf_tensor("W_sb", [128, NOT, NE], F32))
        b1T = E(nc.sbuf_tensor("b1T", [128, 128], F32))
        b1_raw = E(nc.sbuf_tensor("b1_raw", [128, 128], F32))
        brout_row = E(nc.sbuf_tensor("brout_row", [1, NE], F32))

        psA = E(nc.psum_tensor("psA", [128, 2048], F32))
        psB = E(nc.psum_tensor("psB", [128, 2048], F32))

        dS = Sem(ctx, nc, "dS")
        dG = Sem(ctx, nc, "dG")
        dO = Sem(ctx, nc, "dO")
        sP = Sem(ctx, nc, "sP")
        sA = Sem(ctx, nc, "sA")
        sV = Sem(ctx, nc, "sV")
        sG = Sem(ctx, nc, "sG")

        blk = E(nc.Block())

        GP.memset(ident[:, :], 0.0)
        GP.drain()
        GP.affine_select(out=ident[:, :], in_=ident[:, :],
                         compare_op=ALU.not_equal, fill=1.0, base=0,
                         pattern=[[-1, 128]], channel_multiplier=1)
        GP.memset(ones1b[:, :], 1.0)
        GP.memset(ones1f[:, :], 1.0)
        GP.memset(ones64b[:, :], 1.0)
        GP.memset(eps_t[:, :], 1e-5)

        dS.dma(SP, brout_row[:, :], b_router[None, :])
        dS.dma(SP, b1_raw[:, :], b1.rearrange("e (q a) -> (e q) a", a=128))
        ev_small = dS.n

        with contextlib.ExitStack() as ab:
            EA = ab.enter_context
            t1kT = EA(nc.sbuf_tensor("t1kT", [128, S], BF16))
            t1vT = EA(nc.sbuf_tensor("t1vT", [128, S], BF16))
            t1qT = EA(nc.sbuf_tensor("t1qT", [128, OWN], BF16))
            fcq_aug = EA(nc.sbuf_tensor("fcq_aug", [128, D], BF16))
            fck_aug = EA(nc.sbuf_tensor("fck_aug", [128, D], BF16))
            fcv_aug = EA(nc.sbuf_tensor("fcv_aug", [128, D], BF16))
            GP.memset(t1kT[64:128, :], 0.0)
            GP.memset(t1vT[64:128, :], 0.0)
            GP.memset(t1qT[64:128, :], 0.0)
            GP.memset(fcq_aug[64:128, :], 0.0)
            GP.memset(fck_aug[64:128, :], 0.0)
            GP.memset(fcv_aug[64:128, :], 0.0)
            kT_2 = EA(nc.sbuf_tensor("kT_2", [128, 2, S], BF16))
            qT_2 = EA(nc.sbuf_tensor("qT_2", [128, 2, OWN], BF16))
            V_pair = EA(nc.sbuf_tensor("V_pair", [128, NT, 130], BF16))
            fcv_pack = EA(nc.sbuf_tensor("fcv_pack", [128, 8, 130], BF16))
            GP.memset(kT_2[64:128, :, :], 0.0)
            GP.memset(qT_2[64:128, :, :], 0.0)
            GP.drain()
            GP.memset(t1kT[64:65, :], 1.0)
            GP.memset(t1vT[64:65, :], 1.0)
            GP.memset(kT_2[64:65, :, :], -1.0)
            ev_consts = sG.inc(GP.memset(t1qT[64:65, :], 1.0))

            with contextlib.ExitStack() as aa:
                EB = aa.enter_context
                wqkv_bf = EB(nc.sbuf_tensor("wqkv_bf", [128, NCH, 4 * DK], BF16))
                w_rout_f = EB(nc.sbuf_tensor("w_rout_f", [128, NCH, NE], F32))
                xg = EB(nc.sbuf_tensor("xg", [128, 2, D], F32))
                xn_f = EB(nc.sbuf_tensor("xn_f", [128, 2, D], F32))
                xnT_st = EB(nc.sbuf_tensor("xnT_st", [128, 2, NCH, 512], BF16))
                xnTf_st = EB(nc.sbuf_tensor("xnTf_st", [128, 2, NCH, 128], F32))
                mvst = EB(nc.sbuf_tensor("mvst", [128, 12], F32))
                lnt = EB(nc.sbuf_tensor("lnt", [128, 2, 6], F32))
                xc_f = EB(nc.sbuf_tensor("xc_f", [128, 2, D], F32))
                rl_sc = EB(nc.sbuf_tensor("rl_sc", [128, 8, NE], F32))
                rst = EB(nc.sbuf_tensor("rst", [128, 4], F32))

                GP.memset(wqkv_bf[:, :, DK:2 * DK], 0.0)
                GP.memset(fcv_pack[:, :, 64:65], 0.0)
                GP.memset(fcv_pack[:, :, 129:130], 0.0)
                GP.drain()
                GP.memset(fcv_pack[64:65, :, 64:65], 1.0)
                ev_fvp0 = sG.inc(GP.memset(fcv_pack[64:65, :, 129:130], 1.0))
                dG.dma(GP, wqkv_bf[:, :, 0:DK], wq.rearrange("(c p) k -> p c k", p=128))
                dG.dma(GP, wqkv_bf[:, :, 2 * DK:3 * DK], wk.rearrange("(c p) k -> p c k", p=128))
                dG.dma(GP, wqkv_bf[:, :, 3 * DK:4 * DK], wv.rearrange("(c p) k -> p c k", p=128))
                ev_wqkv = dG.n
                for wsrc, bsrc, dst in ((fcq_w, fcq_b, fcq_aug), (fck_w, fck_b, fck_aug),
                                        (fcv_w, fcv_b, fcv_aug)):
                    dG.dma(GP, dst[0:64, :], wsrc[:, :])
                    dG.dma(GP, dst[64:65, :], bsrc[None, :])
                ev_wA = dG.n
                ev_wrout = dS.dma(SP, w_rout_f[:, :, :],
                                  w_router.rearrange("(c p) e -> p c e", p=128))

                ev_xdma = [0] * NT
                ev_xn = [0] * NT
                ev_sub = [0] * NT
                ev_tp = [0] * NT
                ev_evac = [0] * NT
                ev_evacf = [0] * NT
                ev_t1 = [0] * NG
                ev_t1ev = [0] * NG
                ev_rlog = [0] * NOT
                ev_rW = [0] * NOT

                def stage1(t):
                    own = t < NOT
                    if t == 0:
                        ACT.wait_ge(sG.h, ev_consts)
                    if t >= 2:
                        SP.wait_ge(sV.h, ev_sub[t - 2])
                    ev_xdma[t] = dS.dma(SP, xg[:, t % 2, :], xp[:, t, :])
                    xtile = xg[:, t % 2, :]
                    lb = lnt[:, t % 2, :]
                    DVE.wait_ge(dS.h, ev_xdma[t])
                    if t >= 2:
                        DVE.wait_ge(sA.h, ev_xn[t - 2])
                    if t > 0:
                        DVE.drain()
                    DVE.bn_stats(mvst[:, 0:6], xtile[:, 0:512])
                    DVE.bn_stats(mvst[:, 6:12], xtile[:, 512:1024])
                    DVE.drain()
                    ev1 = sV.inc(DVE.bn_aggr(lb[:, 0:2], mvst[:, 0:12]))
                    DVE.drain()
                    ev_sub[t] = sV.inc(DVE.tensor_scalar(
                        out=xc_f[:, t % 2, :], in0=xtile,
                        scalar1=lb[:, 0:1], scalar2=None, op0=ALU.subtract))
                    ACT.wait_ge(sV.h, ev1)
                    if t > 0:
                        ACT.drain()
                    ACT.activation(lb[:, 4:5], lb[:, 1:2], AF.Ln, bias=eps_t[:, :])
                    ACT.drain()
                    ACT.activation(lb[:, 2:3], lb[:, 4:5], AF.Exp, scale=-0.5)
                    ACT.drain()
                    ACT.wait_ge(sV.h, ev_sub[t])
                    ev_xn[t] = sA.inc(ACT.activation(xn_f[:, t % 2, :],
                                                     xc_f[:, t % 2, :],
                                                     AF.Identity, scale=lb[:, 2:3]))

                def stage2(t):
                    own = t < NOT
                    g4 = t // 4
                    if t == 0:
                        PE.wait_ge(sG.h, ev_consts)
                    PE.wait_ge(sA.h, ev_xn[t])
                    if t >= 2:
                        if t - 2 < NOT:
                            PE.wait_ge(sA.h, ev_evacf[t - 2])
                        else:
                            PE.wait_ge(sV.h, ev_evac[t - 2])
                    base = (t % 2) * 1024
                    for c in range(NCH):
                        off = base + (c // 4) * 512 + (c % 4) * 128
                        ins = PE.transpose(psA[:, off:off + 128],
                                           xn_f[:, t % 2, c * 128:(c + 1) * 128],
                                           ident[:, :])
                    ev_tp[t] = sP.inc(ins)
                    if own:
                        ACT.wait_ge(sP.h, ev_tp[t])
                        for j in range(2):
                            ins = ACT.activation(
                                xnTf_st[:, t % 2, j * 4:(j + 1) * 4, :],
                                psA[:, base + j * 512: base + j * 512 + 512], AF.Copy)
                        ev_evacf[t] = sA.inc(ins)
                        DVE.wait_ge(sA.h, ev_evacf[t])
                        ins = DVE.tensor_copy(
                            xnT_own[:, :, t * 128:(t + 1) * 128],
                            xnTf_st[:, t % 2, :, :])
                        ev_evac[t] = sV.inc(ins)
                    else:
                        DVE.wait_ge(sP.h, ev_tp[t])
                        if g4 >= 2:
                            DVE.wait_ge(sP.h, ev_t1[g4 - 2])
                        for j in range(2):
                            srcp = psA[:, base + j * 512: base + j * 512 + 512]
                            dst = xnT_st[:, g4 % 2, j * 4:(j + 1) * 4,
                                         (t % 4) * 128:(t % 4) * 128 + 128]
                            ins = DVE.tensor_copy(dst, srcp)
                        ev_evac[t] = sV.inc(ins)
                        ev_evacf[t] = 0
                    if t % 4 == 3:
                        PE.wait_ge(sV.h, ev_evac[t])
                        if g4 == 0:
                            PE.wait_ge(dG.h, ev_wqkv)
                        if g4 >= 1:
                            PE.wait_ge(sA.h, ev_t1ev[g4 - 1])
                        projs = [(2, psB[:, 0:512])]
                        if own:
                            projs.append((0, psB[:, 512:1024]))
                        for wi, ps in projs:
                            for c in range(NCH):
                                if own:
                                    rhs = xnT_own[:, c, g4 * 512:(g4 + 1) * 512]
                                else:
                                    rhs = xnT_st[:, g4 % 2, c, :]
                                ins = PE.matmul(ps, wqkv_bf[:, c, wi * DK:(wi + 2) * DK],
                                                rhs, start=(c == 0), stop=(c == NCH - 1))
                        ev_t1[g4] = sP.inc(ins)
                        ACT.wait_ge(sP.h, ev_t1[g4])
                        ins = ACT.activation(t1kT[0:64, g4 * 512:(g4 + 1) * 512],
                                             psB[0:64, 0:512], AF.Copy)
                        ins = ACT.activation(t1vT[0:64, g4 * 512:(g4 + 1) * 512],
                                             psB[64:128, 0:512], AF.Copy)
                        if own:
                            ins = ACT.activation(t1qT[0:64, g4 * 512:(g4 + 1) * 512],
                                                 psB[0:64, 512:1024], AF.Copy)
                        ev_t1ev[g4] = sA.inc(ins)

                def stage3(t):
                    slot = t % 4
                    rps = psB[:, 1536 + slot * NE: 1536 + (slot + 1) * NE]
                    PE.wait_ge(sA.h, ev_evacf[t])
                    if t == 0:
                        PE.wait_ge(dS.h, ev_wrout)
                    else:
                        PE.wait_ge(sV.h, ev_rlog[t - 1])
                    PE.matmul(rps, ones1f[:, :], brout_row[:, :], start=True, stop=False)
                    for c in range(NCH):
                        ins = PE.matmul(rps, xnTf_st[:, t % 2, c, :], w_rout_f[:, c, :],
                                        start=False, stop=(c == NCH - 1))
                    evr = sP.inc(ins)
                    DVE.wait_ge(sP.h, evr)
                    if t >= 4:
                        DVE.wait_ge(sV.h, ev_rW[t - 4])
                    DVE.drain()
                    DVE.tensor_copy(rl_sc[:, slot, :], rps)
                    DVE.drain()
                    DVE.reduce_max(rst[:, 0:1], rl_sc[:, slot, :], axis=AX.X)
                    DVE.drain()
                    ev_rlog[t] = sV.inc(DVE.tensor_scalar(
                        out=rst[:, 1:2], in0=rst[:, 0:1],
                        scalar1=-1.0, scalar2=None, op0=ALU.mult))
                    ACT.wait_ge(sV.h, ev_rlog[t])
                    ACT.drain()
                    ev5 = sA.inc(ACT.activation(rl_sc[:, 4 + slot, :], rl_sc[:, slot, :],
                                                AF.Exp, bias=rst[:, 1:2]))
                    DVE.wait_ge(sA.h, ev5)
                    DVE.drain()
                    DVE.reduce_sum(rst[:, 2:3], rl_sc[:, 4 + slot, :], axis=AX.X)
                    DVE.drain()
                    DVE.reciprocal(rst[:, 3:4], rst[:, 2:3])
                    DVE.drain()
                    ev_rW[t] = sV.inc(DVE.tensor_scalar(
                        out=W_sb[:, t, :], in0=rl_sc[:, slot, :],
                        scalar1=rst[:, 0:1], scalar2=rst[:, 3:4],
                        op0=ALU.is_equal, op1=ALU.mult))

                for t in range(NT + 2):
                    if t < NT:
                        stage1(t)
                    if 1 <= t <= NT:
                        stage2(t - 1)
                    if 2 <= t and t - 2 < NOT:
                        stage3(t - 2)

                last_t1ev = ev_t1ev[NG - 1]
                last_A_evac = ev_evac[NT - 1]

                if phases in (1, 9, 10, 11, 12, 13):
                    if phases in (9, 10):
                        SP.wait_ge(sA.h, sA.n)
                        SP.wait_ge(sV.h, sV.n)
                        return nc
                    SP.wait_ge(sA.h, sA.n)
                    SP.wait_ge(sV.h, sV.n)
                    return nc

            bb = contextlib.ExitStack()
            EBB = bb.enter_context
            pTf = EBB(nc.sbuf_tensor("pTf", [128, 2048], BF16))
            sq2q = EBB(nc.sbuf_tensor("sq2q", [64, 2, OWN], BF16))
            colmax = EBB(nc.sbuf_tensor("colmax", [64, 2], F32))
            colsq = EBB(nc.sbuf_tensor("colsq", [64, 2], BF16))
            row_t = EBB(nc.sbuf_tensor("row_t", [1, OWN], F32))
            mk2p = EBB(nc.sbuf_tensor("mk2p", [1, 2], F32))
            r_row = EBB(nc.sbuf_tensor("r_row", [1, 512], F32))
            atnU = EBB(nc.sbuf_tensor("atnU", [64, 512], BF16))
            r_den = EBB(nc.sbuf_tensor("r_den", [1, 512], F32))
            rbc_f32 = EBB(nc.sbuf_tensor("rbc_f32", [64, 512], F32))
            atn_st = EBB(nc.sbuf_tensor("atn_st", [64, 2, OWN], BF16))

            NPASS = NH * 4
            NSTEP = NPASS * NT
            ev_qk = [0] * NSTEP
            ev_expp = [0] * (NSTEP // 2)
            ev_av = [0] * NSTEP
            ev_epi = [0] * NPASS
            ev_den = [0] * NPASS
            ev_spill = [0] * NH
            ev_kev = {}
            ev_qev = {}
            ev_row_h = [0] * NH
            ev_nq_h = [0] * NH
            ev_q2g = [0] * (NH // 2)
            ev_cm_h = [0] * (NH // 2)

            DVE.wait_ge(dG.h, ev_wA)
            for p8 in range(8):
                DVE.drain()
                DVE.tensor_copy(fcv_pack[:, p8, 0:64],
                                fcv_aug[:, p8 * 128:p8 * 128 + 64])
                DVE.drain()
                ins = DVE.tensor_copy(fcv_pack[:, p8, 65:129],
                                      fcv_aug[:, p8 * 128 + 64:p8 * 128 + 128])
            ev_fvp = sV.inc(ins)

            def prologue(p):
                h = 2 * p
                s0 = p * 8 * NT
                if p == 0:
                    PE.wait_ge(dG.h, ev_wA)
                    PE.wait_ge(sA.h, last_t1ev)
                    PE.wait_ge(sV.h, last_A_evac)
                else:
                    PE.wait_ge(sA.h, ev_expp[(s0 - 2) // 2])
                    ACT.wait_ge(sV.h, ev_q2g[p - 1])
                hc2 = slice(h * DK, (h + 2) * DK)
                for j in range(16):
                    if j >= 4:
                        if (j - 4) % 4 < 2:
                            PE.wait_ge(sA.h, ev_kev[(h, j - 4)])
                        else:
                            PE.wait_ge(sV.h, ev_kev[(h, j - 4)])
                    ins = PE.matmul(psA[:, (j % 4) * 512:(j % 4) * 512 + 512],
                                    fck_aug[:, hc2],
                                    t1kT[:, j * 512:(j + 1) * 512],
                                    start=True, stop=True)
                    evk = sP.inc(ins)
                    eng = ACT if (j % 4) < 2 else DVE
                    eng.wait_ge(sP.h, evk)
                    if eng is DVE:
                        eng.drain()
                    for pj in range(2):
                        srck = psA[pj * 64:(pj + 1) * 64,
                                   (j % 4) * 512:(j % 4) * 512 + 512]
                        dstk = kT_2[0:64, pj, j * 512:(j + 1) * 512]
                        if eng is ACT:
                            ins2 = eng.activation(dstk, srck, AF.Copy)
                        else:
                            ins2 = eng.tensor_copy(dstk, srck)
                    ev_kev[(h, j)] = (sA if eng is ACT else sV).inc(ins2)
                for jq in range(4):
                    prev = ev_kev[(h, 12 + jq)]
                    if (12 + jq) % 4 < 2:
                        PE.wait_ge(sA.h, prev)
                    else:
                        PE.wait_ge(sV.h, prev)
                    ins = PE.matmul(psA[:, jq * 512:jq * 512 + 512],
                                    fcq_aug[:, hc2],
                                    t1qT[:, jq * 512:(jq + 1) * 512],
                                    start=True, stop=True)
                    evq = sP.inc(ins)
                    eng = ACT if (jq % 4) < 2 else DVE
                    eng.wait_ge(sP.h, evq)
                    if eng is DVE:
                        eng.drain()
                    for parq in range(2):
                        srcq = psA[parq * 64:(parq + 1) * 64,
                                   jq * 512:jq * 512 + 512]
                        dstq = qT_2[0:64, parq, jq * 512:(jq + 1) * 512]
                        if eng is ACT:
                            ins = eng.activation(dstq, srcq, AF.Copy)
                        else:
                            ins = eng.tensor_copy(dstq, srcq)
                    ev_qev[(h, jq)] = (sA if eng is ACT else sV).inc(ins)

                ev_vev = [0] * NT
                for kt in range(NT):
                    sl = (kt % 4) * 512
                    if kt == 0:
                        PE.wait_ge(sA.h, ev_qev[(h, 1)])
                        PE.wait_ge(sV.h, ev_qev[(h, 3)])
                        if p == 0:
                            PE.wait_ge(sV.h, ev_fvp)
                            PE.wait_ge(sG.h, ev_fvp0)
                    elif kt >= 4:
                        PE.wait_ge(sA.h if (kt - 4) % 2 else sV.h, ev_vev[kt - 4])
                    ins = PE.matmul(psA[:, sl:sl + 130],
                                    t1vT[:, kt * 128:(kt + 1) * 128],
                                    fcv_pack[:, p, :], start=True, stop=True)
                    evv = sP.inc(ins)
                    eng = ACT if kt % 2 else DVE
                    eng.wait_ge(sP.h, evv)
                    if eng is DVE:
                        eng.drain()
                        ev_vev[kt] = sV.inc(eng.tensor_copy(
                            V_pair[:, kt, :], psA[:, sl:sl + 130]))
                    else:
                        ev_vev[kt] = sA.inc(eng.activation(
                            V_pair[:, kt, :], psA[:, sl:sl + 130], AF.Copy))

                DVE.wait_ge(sA.h, ev_kev[(h, 13)])
                DVE.wait_ge(sV.h, ev_kev[(h, 15)])
                if p > 0:
                    DVE.wait_ge(sP.h, ev_nq_h[h - 1])
                for hh in range(2):
                    DVE.drain()
                    DVE.reduce_max(colmax[:, hh:hh + 1], kT_2[0:64, hh, :],
                                   axis=AX.X, apply_absolute_value=True)
                DVE.drain()
                ev_cm_h[p] = sV.inc(DVE.tensor_tensor(
                    out=colsq[:, :], in0=colmax[:, :], in1=colmax[:, :],
                    op=ALU.mult))
                DVE.wait_ge(sA.h, ev_qev[(h, 1)])
                DVE.wait_ge(sV.h, ev_qev[(h, 3)])
                for hh in range(2):
                    DVE.drain()
                    ins = DVE.tensor_tensor(out=sq2q[:, hh, :],
                                            in0=qT_2[0:64, hh, :],
                                            in1=qT_2[0:64, hh, :], op=ALU.mult)
                ev_q2g[p] = sV.inc(ins)
                PE.wait_ge(sV.h, ev_cm_h[p])
                ins = PE.matmul(psA[0:1, 384:386], ones64b[:, :], colsq[:, :],
                                start=True, stop=True)
                ev_mk = sP.inc(ins)
                DVE.wait_ge(sP.h, ev_mk)
                DVE.drain()
                ev_mkc = sV.inc(DVE.tensor_copy(mk2p[:, :], psA[0:1, 384:386]))
                for hh in range(2):
                    PE.wait_ge(sV.h, ev_q2g[p])
                    if hh == 0:
                        PE.wait_ge(sV.h, ev_mkc)
                        PE.wait_ge(sA.h, ev_vev[NT - 1])
                    else:
                        PE.wait_ge(sA.h, ev_row_h[h])
                    for jq in range(4):
                        ins = PE.matmul(psA[0:1, jq * 512:jq * 512 + 512],
                                        ones64b[:, :],
                                        sq2q[:, hh, jq * 512:(jq + 1) * 512],
                                        start=True, stop=True)
                    ev_nq_h[h + hh] = sP.inc(ins)
                    ACT.wait_ge(sP.h, ev_nq_h[h + hh])
                    if hh == 0:
                        ACT.wait_ge(sV.h, ev_mkc)
                    ACT.activation(row_t[:, :], psA[0:1, 0:2048],
                                   AF.Ln, scale=mk2p[:, hh:hh + 1])
                    ACT.drain()
                    ev_row_h[h + hh] = sA.inc(ACT.activation(
                        qT_2[64:65, hh, :], row_t[:, :], AF.Exp, scale=0.5))

            def emit_qk(s):
                pss, kt = divmod(s, NT)
                head, qp = divmod(pss, 4)
                par = head % 2
                b = s % 4
                if s >= 4:
                    PE.wait_ge(sA.h, ev_expp[(s - 4) // 2])
                if s % (8 * NT) < 4:
                    PE.wait_ge(sA.h, ev_row_h[(s // (8 * NT)) * 2 + 1])
                ins = PE.matmul(psA[:, b * 512:(b + 1) * 512],
                                kT_2[:, par, kt * 128:(kt + 1) * 128],
                                qT_2[:, par, qp * 512:(qp + 1) * 512],
                                start=True, stop=True)
                ev_qk[s] = sP.inc(ins)

            def emit_expp(j):
                b = (2 * j) % 4
                ACT.wait_ge(sP.h, ev_qk[2 * j + 1])
                ev_expp[j] = sA.inc(ACT.activation(
                    pTf[:, b * 512:b * 512 + 1024],
                    psA[:, b * 512:b * 512 + 1024], AF.Exp, scale=0.125))

            def emit_av(s):
                pss, kt = divmod(s, NT)
                head = pss // 4
                hh = head % 2
                b = pss % 4
                PE.wait_ge(sA.h, ev_expp[s // 2])
                if kt == 0 and pss >= 4:
                    PE.wait_ge(sV.h, ev_epi[pss - 4])
                ins = PE.matmul(psB[0:65, b * 512:(b + 1) * 512],
                                V_pair[:, kt, hh * 65:hh * 65 + 65],
                                pTf[:, (s % 4) * 512:(s % 4) * 512 + 512],
                                start=(kt == 0), stop=(kt == NT - 1))
                ev_av[s] = sP.inc(ins)

            def emit_epiA(pss):
                b = pss % 4
                DVE.wait_ge(sP.h, ev_av[pss * NT + NT - 1])
                DVE.drain()
                DVE.tensor_copy(atnU[:, :], psB[0:64, b * 512:(b + 1) * 512])
                DVE.drain()
                ev_den[pss] = sV.inc(DVE.tensor_copy(
                    r_den[:, :], psB[64:65, b * 512:(b + 1) * 512]))

            def emit_epiB(pss):
                head, qp = divmod(pss, 4)
                par = head % 2
                b = pss % 4
                PE.wait_ge(sV.h, ev_den[pss])
                ins = PE.matmul(psB[0:64, b * 512:(b + 1) * 512],
                                ones1f[:, 0:64], r_den[:, :],
                                start=True, stop=True)
                ev_rbc = sP.inc(ins)
                DVE.wait_ge(sP.h, ev_rbc)
                if head >= 2 and qp == 0:
                    DVE.wait_ge(dO.h, ev_spill[head - 2])
                DVE.drain()
                DVE.reciprocal(rbc_f32[:, :], psB[0:64, b * 512:(b + 1) * 512])
                DVE.drain()
                ev_epi[pss] = sV.inc(DVE.tensor_tensor(
                    out=atn_st[:, par, qp * 512:(qp + 1) * 512],
                    in0=atnU[:, :], in1=rbc_f32[:, :],
                    op=ALU.mult))
                if qp == 3:
                    SP.wait_ge(sV.h, ev_epi[pss])
                    ev_spill[head] = dO.dma(
                        SP, attnT_hbm[head * DK:(head + 1) * DK, :],
                        atn_st[:, par, :])

            for s in range(NSTEP + 8):
                if s < NSTEP and s % (8 * NT) == 0:
                    prologue(s // (8 * NT))
                if s < NSTEP:
                    emit_qk(s)
                if s >= 3 and s - 3 < NSTEP:
                    emit_av(s - 3)
                if s < NSTEP and s % 2 == 1:
                    emit_expp((s - 1) // 2)
                if s >= 3 and (s - 3) % NT == NT - 1:
                    emit_epiA((s - 3) // NT)
                if s >= 8 and (s - 8) % NT == NT - 1:
                    emit_epiB((s - 8) // NT)

            bb.close()
        snapB_P, snapB_A, snapB_V = sP.n, sA.n, sV.n

        with contextlib.ExitStack() as cc:
            EC = cc.enter_context
            out_acc = EC(nc.sbuf_tensor("out_acc", [128, NOT, D], F32))
            w1buf = EC(nc.sbuf_tensor("w1buf", [128, NCH, HF], BF16))
            w2buf = EC(nc.sbuf_tensor("w2buf", [128, NHC, D], BF16))
            hT = EC(nc.sbuf_tensor("hT", [128, NHC, 512], BF16))
            aT_ld = EC(nc.sbuf_tensor("aT_ld", [128, 2, NCH, 128], BF16))
            b2rows = EC(nc.sbuf_tensor("b2rows", [1, NE * D], BF16))
            fco_bb = EC(nc.sbuf_tensor("fco_bb", [128, D], F32))
            fco_brow = EC(nc.sbuf_tensor("fco_brow", [1, D], F32))
            fco_view = w2buf[:, 0:NCH, :]

            SP.wait_ge(dO.h, ev_spill[NH - 1])
            SP.wait_ge(sP.h, snapB_P)
            SP.wait_ge(sV.h, snapB_V)
            dS.dma(SP, out_acc[:, :, :], xp[:, 0:NOT, :])
            ev_xinit = dS.dma(SP, fco_brow[:, :], fco_b[None, :])
            GP.wait_ge(sP.h, snapB_P)
            GP.wait_ge(sV.h, snapB_V)
            GP.wait_ge(dO.h, ev_spill[NH - 1])
            dG.dma(GP, fco_view[:, :, :], fco_w.rearrange("(c p) e -> p c e", p=128))
            dG.dma(GP, b2rows[:, :], b2.rearrange("e d -> (e d)")[None, :])
            ev_fco = dG.n
            ev_b2r = dG.n
            PE.wait_ge(dS.h, ev_xinit)
            PE.wait_ge(sA.h, snapB_A)
            PE.wait_ge(sV.h, snapB_V)
            for half in range(2):
                ins = PE.matmul(psA[:, half * 512:(half + 1) * 512],
                                ones1f[:, :], fco_brow[:, half * 512:(half + 1) * 512],
                                start=True, stop=True)
            ev_bbmm = sP.inc(ins)
            ACT.wait_ge(sP.h, ev_bbmm)
            ev_bbe = sA.inc(ACT.activation(fco_bb[:, :], psA[:, 0:1024], AF.Copy))
            ins = PE.transpose(psA[:, 1024:1152], b1_raw[:, :], ident[:, :])
            ev_b1t = sP.inc(ins)
            ACT.wait_ge(sP.h, ev_b1t)
            ev_b1e = sA.inc(ACT.activation(b1T[:, :], psA[:, 1024:1152], AF.Copy))
            DVE.wait_ge(sA.h, ev_bbe)
            DVE.wait_ge(dS.h, ev_xinit)
            DVE.drain()
            for gt in range(NOT):
                ins = DVE.tensor_tensor(out=out_acc[:, gt, :], in0=out_acc[:, gt, :],
                                        in1=fco_bb[:, :], op=ALU.add)
            ev_init = sV.inc(ins)

            aTs = attnT_hbm.rearrange("(c p) t -> p c t", p=128)
            ev_c0mm = [0] * NOT
            ev_c0ev = [0] * NOT
            ev_ald = [0] * NOT
            for gt in range(NOT):
                if gt >= 2:
                    SP.wait_ge(sP.h, ev_c0mm[gt - 2])
                ev_ald[gt] = dS.dma(SP, aT_ld[:, gt % 2, :, :],
                                    aTs[:, :, gt * 128:(gt + 1) * 128])
                PE.wait_ge(dS.h, ev_ald[gt])
                if gt == 0:
                    PE.wait_ge(dG.h, ev_fco)
                if gt >= 2:
                    PE.wait_ge(sV.h, ev_c0ev[gt - 2])
                for half in range(2):
                    ps = psB[:, (gt % 2) * 1024 + half * 512:
                             (gt % 2) * 1024 + (half + 1) * 512]
                    for c in range(NCH):
                        ins = PE.matmul(ps, aT_ld[:, gt % 2, c, :],
                                        fco_view[:, c, half * 512:(half + 1) * 512],
                                        start=(c == 0), stop=(c == NCH - 1))
                ev_c0mm[gt] = sP.inc(ins)
                DVE.wait_ge(sP.h, ev_c0mm[gt])
                if gt == 0:
                    DVE.wait_ge(sV.h, ev_init)
                DVE.drain()
                for half in range(2):
                    ps = psB[:, (gt % 2) * 1024 + half * 512:
                             (gt % 2) * 1024 + (half + 1) * 512]
                    ins = DVE.tensor_tensor(
                        out=out_acc[:, gt, half * 512:(half + 1) * 512],
                        in0=ps, in1=out_acc[:, gt, half * 512:(half + 1) * 512],
                        op=ALU.add)
                ev_c0ev[gt] = sV.inc(ins)

            if phases == 3:
                SP.wait_ge(sV.h, ev_c0ev[NOT - 1])
                dO.dma(SP, out.rearrange("(t p) f -> p t f", p=128), out_acc[:, :, :])
                SP.wait_ge(dO.h, dO.n)
                return nc

            ev_we = [0] * NE
            ev_w1 = [0] * NE
            ev_hmm_last = [0] * NE
            ev_hev = {}
            ev_ymm = {}
            ev_yev = {}
            last_yev = 0
            for e in range(NE):
                if e == 0:
                    GP.wait_ge(sP.h, ev_c0mm[NOT - 1])
                else:
                    GP.wait_ge(sP.h, ev_hmm_last[e - 1])
                ev_w1[e] = dG.dma(GP, w1buf[:, :, :],
                                  w1[e].rearrange("(c p) h -> p c h", p=128))
                if e > 0:
                    GP.wait_ge(sP.h, ev_ymm[(e - 1, 3)])
                ev_we[e] = dG.dma(GP, w2buf[:, :, :],
                                  w2[e].rearrange("(c p) d -> p c d", p=128))

                for tt in range(4):
                    for Hc in range(NHC):
                        ps = psA[:, (Hc % 4) * 512:(Hc % 4) * 512 + 512]
                        if Hc == 0 and tt == 0:
                            PE.wait_ge(dG.h, ev_w1[e])
                            if e == 0:
                                PE.wait_ge(sA.h, ev_b1e)
                        if Hc >= 4:
                            PE.wait_ge(sA.h, ev_hev[(e, tt, Hc - 4)])
                        elif tt > 0:
                            PE.wait_ge(sA.h, ev_hev[(e, tt - 1, NHC - 4 + Hc)])
                        elif e > 0:
                            PE.wait_ge(sA.h, ev_hev[(e - 1, 3, NHC - 4 + Hc)])
                        for d in range(NCH):
                            ins = PE.matmul(ps, w1buf[:, d, Hc * 128:(Hc + 1) * 128],
                                            xnT_own[:, d, tt * 512:(tt + 1) * 512],
                                            start=(d == 0), stop=(d == NCH - 1))
                        evh = sP.inc(ins)
                        ev_hmm_last[e] = evh
                        ACT.wait_ge(sP.h, evh)
                        if tt > 0:
                            ACT.wait_ge(sP.h, ev_ymm[(e, tt - 1)])
                        elif e > 0:
                            ACT.wait_ge(sP.h, ev_ymm[(e - 1, 3)])
                        ev_hev[(e, tt, Hc)] = sA.inc(ACT.activation(
                            hT[:, Hc, :], ps, AF.Relu,
                            bias=b1T[:, e * NHC + Hc: e * NHC + Hc + 1]))
                    for ttok in range(4):
                        gt = tt * 4 + ttok
                        for fc in range(2):
                            bslot = (ttok * 2 + fc) % 4
                            ps = psB[:, bslot * 512:(bslot + 1) * 512]
                            PE.wait_ge(sA.h, ev_hev[(e, tt, NHC - 1)])
                            if ttok >= 2:
                                pk = ev_yev[(e, tt, ttok - 2, fc)]
                            elif tt > 0:
                                pk = ev_yev[(e, tt - 1, ttok + 2, fc)]
                            elif e > 0:
                                pk = ev_yev[(e - 1, 3, ttok + 2, fc)]
                            else:
                                pk = ev_c0ev[NOT - 1]
                            PE.wait_ge(sV.h, pk)
                            if tt == 0 and ttok == 0 and fc == 0:
                                PE.wait_ge(dG.h, ev_we[e])
                                if e == 0:
                                    PE.wait_ge(dG.h, ev_b2r)
                            PE.matmul(ps, ones1b[:, :],
                                      b2rows[:, e * D + fc * 512: e * D + (fc + 1) * 512],
                                      start=True, stop=False)
                            for Hc in range(NHC):
                                ins = PE.matmul(ps, hT[:, Hc, ttok * 128:(ttok + 1) * 128],
                                                w2buf[:, Hc, fc * 512:(fc + 1) * 512],
                                                start=False, stop=(Hc == NHC - 1))
                            ev_ymm[(e, tt)] = sP.inc(ins)
                            DVE.wait_ge(sP.h, ev_ymm[(e, tt)])
                            DVE.drain()
                            ev_yev[(e, tt, ttok, fc)] = sV.inc(DVE.scalar_tensor_tensor(
                                out=out_acc[:, gt, fc * 512:(fc + 1) * 512],
                                in0=ps, scalar=W_sb[:, gt, e:e + 1],
                                in1=out_acc[:, gt, fc * 512:(fc + 1) * 512],
                                op0=ALU.mult, op1=ALU.add))
                            last_yev = ev_yev[(e, tt, ttok, fc)]

            SP.wait_ge(sV.h, last_yev)
            dO.dma(SP, out.rearrange("(t p) f -> p t f", p=128), out_acc[:, :, :])
            SP.wait_ge(dO.h, dO.n)

    return nc


def kernel(**inputs):
    x_full = np.ascontiguousarray(inputs["L0_attn_z"])
    keymap = {
        "wq": "L0_attn_wq", "wk": "L0_attn_wk", "wv": "L0_attn_wv",
        "fcq_w": "L0_attn_fcq_w", "fcq_b": "L0_attn_fcq_b",
        "fck_w": "L0_attn_fck_w", "fck_b": "L0_attn_fck_b",
        "fcv_w": "L0_attn_fcv_w", "fcv_b": "L0_attn_fcv_b",
        "fco_w": "L0_attn_fco_w", "fco_b": "L0_attn_fco_b",
        "w_router": "L0_ffn_w_router", "b_router": "L0_ffn_b_router",
        "w1": "L0_ffn_w1", "b1": "L0_ffn_b1", "w2": "L0_ffn_w2", "b2": "L0_ffn_b2",
    }
    weights = {k: np.ascontiguousarray(inputs[v]) for k, v in keymap.items()}

    nc = build_nc()
    in_maps = []
    for core in range(NC):
        b, q = core // 4, core % 4
        xb = x_full[b]
        own = xb[q * OWN:(q + 1) * OWN]
        rest = np.concatenate([xb[:q * OWN], xb[(q + 1) * OWN:]], axis=0)
        xc = np.ascontiguousarray(np.concatenate([own, rest], axis=0))
        m = {"x": xc}
        m.update(weights)
        in_maps.append(m)

    res = run_bass_kernel_spmd(nc, in_maps, core_ids=list(range(NC)))
    outp = np.empty((B, S, D), dtype=np.float32)
    for core in range(NC):
        b, q = core // 4, core % 4
        outp[b, q * OWN:(q + 1) * OWN] = res.results[core]["out"]
    return outp
